# version 01
# baseline (speedup 1.0000x reference)
import contextlib
import numpy as np
import ml_dtypes
import concourse.bass as bass
import concourse.mybir as mybir
from concourse.bass_utils import run_bass_kernel_spmd

F32 = mybir.dt.float32
BF16 = mybir.dt.bfloat16
U8 = mybir.dt.uint8
AF = mybir.ActivationFunctionType
ALU = mybir.AluOpType
AXX = mybir.AxisListType.X
BF = ml_dtypes.bfloat16

D = 1024
T = 4096
NCH = 8
J8 = 512
NJ8 = T // J8
J32 = 128
NJ32 = T // J32
ADIM = 256
HID0 = 4096
EH0 = 2048
FF1 = 4096
EPS_OLN = 1e-5
EPS_ILN = 1e-6
EPS_NRM = 1e-12
SCALE_A = 0.25
INVSQD = 1.0 / 32.0

PAGE = 256
HPAGE = 1 << 16

ENGINES = ("tensor", "vector", "scalar", "gpsimd", "sync")
ESEM = {"tensor": "T", "vector": "V", "scalar": "A", "gpsimd": "G"}
DMAQ = {"gpsimd": "gd", "sync": "sd", "scalar": "ad"}
NPOOL = 6
DMAPOOL = tuple(f"{q}{i}" for q in ("gd", "sd", "ad") for i in range(NPOOL))
ALLSEMS = ("T", "V", "A", "G") + DMAPOOL
SELF_SYNC = {"vector", "scalar", "gpsimd", "sync"}


class Prog:
    def __init__(self):
        self.items = {e: [] for e in ENGINES}
        self.cnt = {s: 0 for s in ALLSEMS}
        self.clock = {e: {s: 0 for s in ALLSEMS} for e in ENGINES}
        self.snap = {}
        self.pw = {}
        self.pr = {}
        self.pending = {e: [] for e in ENGINES}
        self.nops = 0
        self.dcur = {"gd": 0, "sd": 0, "ad": 0}
        self.ddirty = {"gd": False, "sd": False, "ad": False}
        self.pend_pages = {e: set() for e in ENGINES}

    @staticmethod
    def pages(spans):
        out = []
        for sp, lo, hi in spans:
            pg = HPAGE if sp.startswith("H:") else (2048 if sp == "PS" else PAGE)
            out.extend((sp, p) for p in range(lo // pg, (hi + pg - 1) // pg))
        return out

    def _wait(self, eng, needed):
        ck = self.clock[eng]
        for sem, val in sorted(needed.items()):
            if ck.get(sem, 0) >= val:
                continue
            self.items[eng].append(("wait", sem, val))
            ck[sem] = val
            sn = self.snap.get((sem, val))
            if sn:
                for s2, v2 in sn.items():
                    if ck.get(s2, 0) < v2:
                        ck[s2] = v2

    def op(self, eng, fn, reads=(), writes=(), inc=True, dma=False):
        rp = self.pages(reads)
        wp = self.pages(writes)
        for e2 in ENGINES:
            if e2 == eng or not self.pend_pages[e2]:
                continue
            for p in wp:
                assert p not in self.pend_pages[e2], (
                    "write to page touched by pending group", eng, e2, p)
            for p in rp:
                assert p not in self.pend_pages[e2], (
                    "read of page written/read by pending group", eng, e2, p)
        needed = {}

        def add(sem, val):
            if val > needed.get(sem, 0):
                needed[sem] = val

        for p in rp:
            lw = self.pw.get(p)
            if lw:
                add(*lw)
        for p in wp:
            lw = self.pw.get(p)
            if lw:
                add(*lw)
            for sem, val in self.pr.get(p, {}).items():
                add(sem, val)
        if eng == "tensor" and not dma and "T" in needed:
            del needed["T"]
        for s in list(needed):
            if s[:2] in ("gd", "sd", "ad") and len(s) == 3:
                needed[s] = self.cnt[s]
                q = s[:2]
                if s == f"{q}{self.dcur[q]}":
                    self.ddirty[q] = True
        self._wait(eng, needed)
        if dma:
            q = DMAQ[eng]
            if self.ddirty[q]:
                idx = (self.dcur[q] + 1) % NPOOL
                s2 = f"{q}{idx}"
                if self.cnt[s2] > 0:
                    self._wait(eng, {s2: self.cnt[s2]})
                self.dcur[q] = idx
                self.ddirty[q] = False
            own = f"{q}{self.dcur[q]}"
        else:
            own = ESEM.get(eng)

        amt = 16 if dma else 1
        self.items[eng].append(("op", fn, own if inc else None, amt if inc else 0))
        self.nops += 1
        if inc:
            assert own is not None
            self.cnt[own] += amt
            v = self.cnt[own]
            ck = self.clock[eng]
            if eng not in SELF_SYNC and ck.get(own, 0) < v:
                ck[own] = v
            self.snap[(own, v)] = dict(ck, **{own: v})
            mark = (own, v)
            todo = self.pending[eng] + [(rp, wp)]
            self.pending[eng] = []
            self.pend_pages[eng] = set()
            for rps, wps in todo:
                for p in wps:
                    self.pw[p] = mark
                    self.pr[p] = {}
                for p in rps:
                    d = self.pr.setdefault(p, {})
                    if v > d.get(own, 0):
                        d[own] = v
        else:
            self.pending[eng].append((rp, wp))
            self.pend_pages[eng].update(rp)
            self.pend_pages[eng].update(wp)

    def finalize(self, eng="sync"):
        assert all(not p for p in self.pending.values())
        self._wait(eng, {s: self.cnt[s] for s in ALLSEMS if self.cnt[s] > 0})

    def replay(self, nc, sems):
        with nc.Block() as block:
            def mk(eng_name):
                def body(eng):
                    for it in self.items[eng_name]:
                        if it[0] == "wait":
                            eng.wait_ge(sems[it[1]], it[2])
                        else:
                            _, fn, sem, amt = it
                            ins = fn(eng)
                            if sem is not None:
                                ins.then_inc(sems[sem], amt)
                return body
            block.tensor(mk("tensor"))
            block.vector(mk("vector"))
            block.scalar(mk("scalar"))
            block.gpsimd(mk("gpsimd"))
            block.sync(mk("sync"))


class Ctx:
    def __init__(self):
        self.sboff = {}
        self.psoff = {}

    def _span(self, table, space, h, idx):
        name = h.name
        off, size = table[name]
        if not idx:
            return (space, off, off + size)
        shape = list(h.shape[1:])
        esz = mybir.dt.size(h.dtype)
        strides = []
        acc = esz
        for d in reversed(shape):
            strides.insert(0, acc)
            acc *= d
        base = 0
        cur = acc
        for k, it in enumerate(idx):
            if it is None:
                break
            st = strides[k]
            if isinstance(it, int):
                base += it * st
                cur = st
            else:
                a, b = it
                base += a * st
                cur = (b - a) * st
        return (space, off + base, off + base + cur)

    def reg(self, h, *idx):
        return self._span(self.sboff, "SB", h, idx)

    def regp(self, h, *idx):
        return self._span(self.psoff, "PS", h, idx)


def hreg2(name, ncol, esz, r0, r1, c0, c1):
    lo = (r0 * ncol + c0) * esz
    hi = ((r1 - 1) * ncol + c1) * esz
    return ("H:" + name, lo, hi)


def hregall(name):
    return ("H:" + name, 0, 1 << 40)


def mm(P, out_ap, lhsT_ap, rhs_ap, reads, wr, start, stop, solo=False):
    P.op("tensor",
         lambda t, o=out_ap, l=lhsT_ap, r=rhs_ap, a=start, b=stop:
             t.matmul(o, l, r, start=a, stop=b),
         reads=reads, writes=(wr,), inc=(stop or solo))


def transp(P, out_ap, in_ap, ident_ap, reads, wr):
    P.op("tensor",
         lambda t, o=out_ap, i=in_ap, d=ident_ap: t.transpose(o, i, d),
         reads=reads, writes=(wr,), inc=True)


def act(P, out_ap, in_ap, func, reads, writes, bias=0.0, scale=1.0, accum=None):
    P.op("scalar",
         lambda s, o=out_ap, i=in_ap, f=func, b=bias, sc=scale, a=accum:
             (s.activation(o, i, f, bias=b, scale=sc) if a is None else
              s.activation(o, i, f, bias=b, scale=sc, accum_out=a)),
         reads=reads, writes=writes)


def vtt(P, out_ap, a_ap, b_ap, opk, reads, writes, eng="vector"):
    P.op(eng,
         lambda v, o=out_ap, a=a_ap, b=b_ap, k=opk: v.tensor_tensor(o, a, b, k),
         reads=reads, writes=writes)


def vts(P, out_ap, in_ap, s1, s2, op0, op1, reads, writes, eng="vector"):
    if s2 is None:
        P.op(eng,
             lambda v, o=out_ap, i=in_ap, a=s1, k0=op0:
                 v.tensor_scalar(o, i, a, None, k0),
             reads=reads, writes=writes)
    else:
        P.op(eng,
             lambda v, o=out_ap, i=in_ap, a=s1, b=s2, k0=op0, k1=op1:
                 v.tensor_scalar(o, i, a, b, k0, k1),
             reads=reads, writes=writes)


def vsts(P, out_ap, in_ap, s_ap, opk, reads, writes, eng="vector"):
    P.op(eng,
         lambda v, o=out_ap, i=in_ap, s=s_ap, k=opk:
             v.tensor_single_scalar(o, i, s, k),
         reads=reads, writes=writes)


def vstt(P, out_ap, in0_ap, scal, in1_ap, op0, op1, reads, writes, eng="vector"):
    P.op(eng,
         lambda v, o=out_ap, i0=in0_ap, s=scal, i1=in1_ap, k0=op0, k1=op1:
             v.scalar_tensor_tensor(o, i0, s, i1, k0, k1),
         reads=reads, writes=writes)


def vred(P, out_ap, in_ap, opk, reads, writes, negate=False):
    P.op("vector",
         lambda v, o=out_ap, i=in_ap, k=opk, n=negate:
             v.tensor_reduce(o, i, AXX, k, negate=n),
         reads=reads, writes=writes)


def vcopy(P, out_ap, in_ap, reads, writes, eng="vector"):
    P.op(eng, lambda v, o=out_ap, i=in_ap: v.tensor_copy(o, i),
         reads=reads, writes=writes)


def vrecip(P, out_ap, in_ap, reads, writes):
    P.op("vector", lambda v, o=out_ap, i=in_ap: v.reciprocal(o, i),
         reads=reads, writes=writes)


def dma(P, eng, out_ap, in_ap, reads, writes, accum=False, slow=False):
    kw = {}
    if accum:
        kw["accum_op"] = ALU.add
    if slow:
        kw["allow_slow_non_contiguous"] = True
    P.op(eng,
         lambda g, o=out_ap, i=in_ap, k=dict(kw): g.dma_start(out=o, in_=i, **k),
         reads=reads, writes=writes, dma=True)


COLSPEC = [
    ("l0_ln1_g", 8, 4), ("l0_ln1_b", 8, 4),
    ("l0_ln2_g", 8, 4), ("l0_ln2_b", 8, 4),
    ("l0_bqk", 4, 4),
    ("l0_bp", 2, 4), ("l0_bf", 8, 4),
    ("l0_b1", 32, 4), ("l0_b2", 8, 4),
    ("l1_bq", 8, 1), ("l1_bk", 8, 1),
    ("l1_sh_b1", 32, 1), ("l1_sh_b2", 8, 1),
    ("l1_sp_b1", 32, 8), ("l1_sp_b2", 8, 8),
]


def build_colmap():
    colmap, colspan = {}, {}
    off = 0
    for name, span, cnt in COLSPEC:
        colmap[name] = off
        colspan[name] = span
        off += span * cnt
    return colmap, colspan, off


def build_ctx(nc, c, taps=()):
    colmap, colspan, ncols = build_colmap()
    c.colmap, c.colspan, c.NCOLS = colmap, colspan, ncols

    def din(name, shape, dtype=F32):
        return nc.dram_tensor(name, list(shape), dtype, kind="ExternalInput").ap()

    c.x_in = din("x", [T, D])
    c.w_l0_Wq = din("l0_Wq", [4, D, ADIM])
    c.w_l0_Wk = din("l0_Wk", [4, D, ADIM])
    c.w_l0_Wp = din("l0_Wp", [4, ADIM, ADIM])
    c.w_l0_Wf = din("l0_Wf", [4, ADIM, D])
    c.w_l0_W1 = din("l0_W1", [4, D, HID0])
    c.w_l0_W2 = din("l0_W2", [4, HID0, D])
    c.w_f0_rwT = din("f0_rwT", [D, 8])
    c.w_f0_w1 = din("f0_w1", [8, D, EH0])
    c.w_f0_w2 = din("f0_w2", [8, EH0, D])
    c.w_l1_Wq = din("l1_Wq", [D, D])
    c.w_l1_Wk = din("l1_Wk", [D, D])
    c.w_f1_rwT = din("f1_rwT", [D, 8])
    c.w_f1_w1T = din("f1_w1T", [9, D, FF1])
    c.w_f1_w2T = din("f1_w2T", [9, FF1, D])
    c.cols_in = din("cols", [128, ncols])
    c.wg_in = din("l0_wg_cols", [128, 8])
    c.spb2_in = din("f1_spb2", [8, D])
    c.rb_in = din("f1_rb", [1, 8])

    c.out = nc.dram_tensor("out", [T, D], F32, kind="ExternalOutput").ap()
    kind = c.scratch_kind
    c.acc_fm = nc.dram_tensor("acc_fm", [D, T], F32, kind=kind).ap()
    c.acc_tm = nc.dram_tensor("acc_tm", [T, D], F32, kind=kind).ap()
    scr = getattr(c, "scr_kind", "Internal")
    c.q_scr = nc.dram_tensor("q_scr", [D, T], BF16, kind="Internal").ap()
    c.arow_scr = nc.dram_tensor("arow_scr", [1, T], BF16, kind="Internal").ap()
    c.xnfm_scr = nc.dram_tensor("xnfm_scr", [D, T], BF16, kind=scr).ap()

    c.taps = {}
    for tap in taps:
        c.taps[tap] = nc.dram_tensor("tap_" + tap, [D, T], F32,
                                     kind="ExternalOutput").ap()

    c.ident_d = nc.inline_tensor(np.eye(128, dtype=BF), "identc")
    c.ident32_d = nc.inline_tensor(np.eye(128, dtype=np.float32), "identc32")
    c.ones_d = nc.inline_tensor(np.ones((128, 1), dtype=BF), "onescol_c")
    c.onesr_d = nc.inline_tensor(np.ones((1, 128), dtype=BF), "onesrow_c")

    def salloc(name, shape, dtype):
        h = nc.alloc_sbuf_tensor(name, list(shape), dtype)
        mloc = nc.lookup_mloc(h)
        c.sboff[h.name] = (mloc.addr, int(np.prod(shape[1:])) * mybir.dt.size(dtype))
        return h

    c.ident = salloc("ident", [128, 128], BF16)
    c.ident32 = salloc("ident32", [128, 128], F32)
    c.onescol = salloc("onescol", [128, 1], BF16)
    c.onesrow = salloc("onesrow", [1, 128], BF16)
    c.constf = salloc("constf", [128, ncols], F32)
    c.wgcol = salloc("wgcol", [128, 4, 2], BF16)
    c.spb2 = salloc("spb2", [8, D], BF16)
    c.rbb = salloc("rbb", [128, 8], F32)

    usable = nc.sbuf_bytes_remaining - 64
    c.arena = nc.alloc_sbuf_tensor("arena", [128, usable], U8)
    c.abase = nc.lookup_mloc(c.arena).addr

    def place(name, shape, dtype, off):
        per = int(np.prod(shape[1:])) * mybir.dt.size(dtype)
        assert off + per <= usable, (name, off, per, usable)
        h = nc.virtual_sbuf_tensor(name, list(shape), dtype)
        nc._tensor(h.name, h.shape, h.dtype, type="SB")
        mloc = nc.lookup_mloc(h)
        mloc.allocated = True
        mloc.addr = c.abase + off
        c.sboff[h.name] = (c.abase + off, per)
        return h

    c.place = place
    K = 1024
    c.zf = place("zf", [128, NCH, T], F32, 0)
    c.qkb = place("qkb", [128, 4, T], BF16, 128 * K)
    c.h2b = place("h2b", [128, NCH, T // 2], BF16, 128 * K)
    c.sc = place("sc", [128, 1536], F32, 160 * K)
    c.sq = place("sq", [128, 512], BF16, 166 * K)
    c.qk_sc = place("qk_sc", [128, 2, 2, J8], BF16, 167 * K)
    c.zbt = place("zbt", [128, 2, J8], BF16, 167 * K)
    c.ht = place("ht", [128, 1, NCH, J8], BF16, 171 * K)
    c.kg = place("kg", [128, 2, J8], BF16, 171 * K)
    c.outt = place("outt", [128, 2, J8], BF16, 173 * K)
    c.f1 = place("f1", [128, 4, J8], BF16, 175 * K)
    c.wqk = place("wqk", [128, NCH, 2 * ADIM], BF16, 179 * K)
    c.w1g = place("w1g", [128, NCH, 512], BF16, 179 * K)
    c.ltsq = place("ltsq", [128, 1, D], F32, 179 * K)
    c.wpf = place("wpf", [128, 2, ADIM], BF16, 187 * K)
    c.wpf2 = place("wpf2", [128, 2, D], BF16, 188 * K)
    c.w2g = place("w2g", [128, 4, D], BF16, 187 * K)
    c.lt = place("lt", [128, 1, D], F32, 195 * K)
    c.segrow = place("segrow", [1, 2, J8], BF16, 195 * K)
    c.rinvrow = place("rinvrow", [1, J8], BF16, 197 * K)
    c.aseg = place("aseg", [1, J8], BF16, 198 * K)
    c.ltstat = place("ltstat", [128, 2, 4], F32, 183 * K)
    c.gpart = place("gpart", [128, 2, NJ8], F32, 199 * K)
    c.gcol = place("gcol", [128, 2], F32, 199 * K + 128)
    c.st_one = place("st_one", [1, 16], F32, 199 * K + 256)
    c.l0_end = 200 * K
    c.xnh = place("xnh", [128, NCH, T // 2], BF16, 0)
    c.w1e = place("w1e", [128, 2, NCH, EH0], BF16, 32 * K)
    c.w2e = place("w2e", [128, 16, D], BF16, 96 * K)
    c.f1e = place("f1e", [128, 2, 16, J8], BF16, 128 * K)
    c.stg = place("stg", [128, 2, 4, J8], F32, 160 * K)
    c.swem = place("swem", [8, T // 2], BF16, 176 * K)
    c.swrow = place("swrow", [1, T // 2], BF16, 180 * K)
    c.rt = place("rt", [128, 32], F32, 184 * K)
    c.ffnsc = place("ffnsc", [128, 2, J8], F32, 185 * K)
    c.ltb2 = place("ltb2", [128, D], BF16, 189 * K)
    c.ltsq2 = place("ltsq2", [128, D], F32, 191 * K)
    c.ltstat2 = place("ltstat2", [128, 8], F32, int(199.5 * K))
    c.rwT = place("rwT", [128, NCH, 8], BF16, int(199.75 * K))
    c.xn_tm = place("xn_tm", [128, NJ32, D], BF16, 0)
    c.k_em = place("k_em", [128, NCH, T], BF16, 64 * K)
    c.wqk1 = place("wqk1", [128, 2, NCH, D], BF16, 128 * K)
    c.xnfm_t = place("xnfm_t", [128, 2, NCH, J8], BF16, 160 * K)
    c.lt1 = place("lt1", [128, D], F32, 176 * K)
    c.af1 = place("af1", [128, NCH, J32], F32, 180 * K)
    c.ltb1 = place("ltb1", [128, D], BF16, 184 * K)
    c.ltsq1 = place("ltsq1", [128, D], F32, 186 * K)
    c.lstat1 = place("lstat1", [128, 8], F32, 190 * K)
    c.stq = place("stq", [128, 2, D], BF16, int(190.5 * K))
    c.csum_r = place("csum_r", [1, D], F32, int(194.5 * K))
    c.csum_b = place("csum_b", [1, D], BF16, int(198.5 * K))
    c.S = place("S", [128, T], F32, 128 * K)
    c.Eb = place("Eb", [128, T], BF16, 144 * K)
    c.qsl = place("qsl", [128, 2, NCH, J32], BF16, 152 * K)
    c.ebt = place("ebt", [128, 2, J32], BF16, 156 * K)
    c.csb = place("csb", [128, D], F32, 160 * K)
    c.sta = place("sta", [128, 2, D], F32, 164 * K)
    c.acol = place("acol", [128, 16], F32, 172 * K)
    c.w1T = place("w1T", [128, NCH, FF1], BF16, 0)
    c.w2T = place("w2T", [128, 32, D], BF16, 64 * K)
    c.xnq = place("xnq", [128, NCH, D], BF16, 128 * K)
    c.F1f = place("F1f", [128, 32, J8], BF16, 144 * K)
    c.stg1 = place("stg1", [128, 2, 2, J8], F32, 176 * K)
    c.wem1 = place("wem1", [8, D], BF16, 184 * K)
    c.wrow1 = place("wrow1", [1, D], BF16, 186 * K)
    c.rt1 = place("rt1", [128, 32], F32, 188 * K)
    c.fsc = place("fsc", [128, 2, J8], F32, 189 * K)
    c.rwT1 = place("rwT1", [128, NCH, 8], BF16, 193 * K)
    c.rbrow = place("rbrow", [1, 8], BF16, int(193.5 * K))

    def palloc(name, shape, dtype):
        h = nc.alloc_psum_tensor(name, list(shape), dtype)
        mloc = nc.lookup_mloc(h)
        c.psoff[h.name] = (mloc.bank * 2048,
                           int(np.prod(shape[1:])) * mybir.dt.size(dtype))
        return h

    c.ps_main = palloc("ps_main", [128, 4, 512], F32)
    c.ps_aux = palloc("ps_aux", [128, 2, 512], F32)
    c.ps_tr32 = palloc("ps_tr32", [128, 2, 512], F32)
    h = nc.virtual_psum_tensor("ps_trb", [128, 2, 1024], BF16)
    nc._tensor(h.name, h.shape, h.dtype, type="PSUM")
    mloc = nc.lookup_mloc(h)
    mloc.allocated = True
    mloc.bank = nc.lookup_mloc(c.ps_tr32).bank
    c.psoff[h.name] = c.psoff[c.ps_tr32.name]
    c.ps_trb = h

    def ccol(name, i, ci=0):
        base = c.colmap[name] + i * c.colspan[name]
        return c.constf[:, base + ci:base + ci + 1]

    c.ccol = ccol
    c.r_const = c.reg(c.constf)
    return c


def rZ(c, ci, j=None):
    if j is None:
        return c.reg(c.zf, ci)
    return c.reg(c.zf, ci, (j * J8, (j + 1) * J8))


def emit_bcast(P, c, row_ap, row_rd, ps_ap, ps_wr):
    mm(P, ps_ap, c.onesrow[0:1, :], row_ap, (row_rd, c.reg(c.onesrow)),
       ps_wr, start=True, stop=True)


def emit_seg_stats(P, c, j, srcb, nchunks, eps, denom):
    RPM2 = c.regp(c.ps_main, 2)
    RPM3 = c.regp(c.ps_main, 3)
    RSQ = c.reg(c.sq)
    r0 = c.sc[0:1, 0:512]
    r1 = c.sc[0:1, 512:1024]
    r2 = c.sc[0:1, 1024:1536]
    RR = c.reg(c.sc)
    for ci in range(nchunks):
        apc, rds = srcb(ci)
        mm(P, c.ps_main[0:1, 2, :], c.onescol[:, 0:1], apc,
           tuple(rds) + (c.reg(c.onescol),), RPM2,
           start=(ci == 0), stop=(ci == nchunks - 1), solo=True)
        act(P, c.sq[:, :], apc, AF.Square, rds, (RSQ,))
        mm(P, c.ps_main[0:1, 3, :], c.onescol[:, 0:1], c.sq[:, :],
           (RSQ, c.reg(c.onescol)), RPM3,
           start=(ci == 0), stop=(ci == nchunks - 1), solo=True)
    vts(P, r0, c.ps_main[0:1, 2, :], 1.0 / denom, None, ALU.mult, ALU.mult,
        (RPM2,), (RR,))
    vts(P, r1, c.ps_main[0:1, 3, :], 1.0 / denom, None, ALU.mult, ALU.mult,
        (RPM3,), (RR,))
    vtt(P, r2, r0, r0, ALU.mult, (RR,), (RR,))
    vtt(P, r1, r1, r2, ALU.subtract, (RR,), (RR,))
    vts(P, r1, r1, float(eps), None, ALU.add, ALU.add, (RR,), (RR,))
    act(P, r1, r1, AF.Sqrt, (RR,), (RR,))
    vrecip(P, r1, r1, (RR,), (RR,))
    RSEG = c.reg(c.segrow)
    act(P, c.segrow[0:1, 0, :], r0, AF.Copy, (RR,), (RSEG,))
    act(P, c.segrow[0:1, 1, :], r1, AF.Copy, (RR,), (RSEG,))


def emit_ln_apply(P, c, j, ci, dst_ap, dst_wr, gname, bname, li):
    jss = slice(j * J8, (j + 1) * J8)
    RPA0 = c.regp(c.ps_aux, 0)
    RPA1 = c.regp(c.ps_aux, 1)
    RSEG = c.reg(c.segrow)
    if ci == 0:
        emit_bcast(P, c, c.segrow[0:1, 0, :], RSEG, c.ps_aux[:, 0, :], RPA0)
        emit_bcast(P, c, c.segrow[0:1, 1, :], RSEG, c.ps_aux[:, 1, :], RPA1)
    RSC0 = c.reg(c.sc, (0, 512))
    RSC1 = c.reg(c.sc, (512, 1024))
    vtt(P, c.sc[:, 0:512], c.zf[:, ci, jss], c.ps_aux[:, 0, :], ALU.subtract,
        (rZ(c, ci, j), RPA0), (RSC0,))
    vtt(P, c.sc[:, 512:1024], c.sc[:, 0:512], c.ps_aux[:, 1, :], ALU.mult,
        (RSC0, RPA1), (RSC1,))
    if gname is None:
        act(P, dst_ap, c.sc[:, 512:1024], AF.Copy, (RSC1,), (dst_wr,))
    else:
        vts(P, dst_ap, c.sc[:, 512:1024], c.ccol(gname, li, ci),
            c.ccol(bname, li, ci), ALU.mult, ALU.add,
            (RSC1, c.r_const), (dst_wr,))


def phase_load_consts(P, c):
    dma(P, "sync", c.ident.ap(), c.ident_d.ap(), (), (c.reg(c.ident),))
    dma(P, "sync", c.ident32.ap(), c.ident32_d.ap(), (), (c.reg(c.ident32),))
    dma(P, "sync", c.onescol.ap(), c.ones_d.ap(), (), (c.reg(c.onescol),))
    dma(P, "sync", c.onesrow.ap(), c.onesr_d.ap(), (), (c.reg(c.onesrow),))
    dma(P, "sync", c.constf.ap(), c.cols_in, (), (c.r_const,))
    dma(P, "gpsimd", c.wgcol.ap(),
        c.wg_in.rearrange("p (a b) -> p a b", b=2), (), (c.reg(c.wgcol),))
    dma(P, "gpsimd", c.spb2.ap(), c.spb2_in, (), (c.reg(c.spb2),))


def phase_load_ln0(P, c, sub=4):
    for j in range(NJ8):
        dma(P, "sync", c.acc_tm[j * J8:(j + 1) * J8, :],
            c.x_in[j * J8:(j + 1) * J8, :], (),
            (hreg2("acc_tm", D, 4, j * J8, (j + 1) * J8, 0, D),))
    if sub < 1:
        return
    for j in range(NJ32):
        pj = 0
        RT = c.reg(c.lt, pj)
        RSQ2 = c.reg(c.ltsq, pj)
        RSTT = c.reg(c.ltstat, pj)
        dma(P, "sync", c.lt[:, pj, :], c.x_in[j * J32:(j + 1) * J32, :], (), (RT,))
        if sub < 2:
            continue
        vred(P, c.ltstat[:, pj, 0:1], c.lt[:, pj, :], ALU.add, (RT,), (RSTT,))
        act(P, c.ltsq[:, pj, :], c.lt[:, pj, :], AF.Square, (RT,), (RSQ2,))
        vred(P, c.ltstat[:, pj, 1:2], c.ltsq[:, pj, :], ALU.add, (RSQ2,), (RSTT,))
        vts(P, c.ltstat[:, pj, 0:1], c.ltstat[:, pj, 0:1], 1.0 / D, None,
            ALU.mult, ALU.mult, (RSTT,), (RSTT,))
        vts(P, c.ltstat[:, pj, 1:2], c.ltstat[:, pj, 1:2], 1.0 / D, None,
            ALU.mult, ALU.mult, (RSTT,), (RSTT,))
        vtt(P, c.ltstat[:, pj, 2:3], c.ltstat[:, pj, 0:1], c.ltstat[:, pj, 0:1],
            ALU.mult, (RSTT,), (RSTT,))
        vtt(P, c.ltstat[:, pj, 2:3], c.ltstat[:, pj, 1:2], c.ltstat[:, pj, 2:3],
            ALU.subtract, (RSTT,), (RSTT,))
        vts(P, c.ltstat[:, pj, 2:3], c.ltstat[:, pj, 2:3], EPS_OLN, None,
            ALU.add, ALU.add, (RSTT,), (RSTT,))
        act(P, c.ltstat[:, pj, 2:3], c.ltstat[:, pj, 2:3], AF.Sqrt,
            (RSTT,), (RSTT,))
        vrecip(P, c.ltstat[:, pj, 3:4], c.ltstat[:, pj, 2:3], (RSTT,), (RSTT,))
        vts(P, c.lt[:, pj, :], c.lt[:, pj, :], c.ltstat[:, pj, 0:1],
            c.ltstat[:, pj, 3:4], ALU.subtract, ALU.mult, (RT, RSTT), (RT,))
        if sub < 3:
            continue
        for ci in range(NCH):
            pb = (j * NCH + ci) % 2
            RPT = c.regp(c.ps_tr32, pb)
            transp(P, c.ps_tr32[:, pb, 0:128], c.lt[:, pj, ci * 128:(ci + 1) * 128],
                   c.ident32[:, :], (RT, c.reg(c.ident32)), RPT)
            vcopy(P, c.zf[:, ci, j * J32:(j + 1) * J32], c.ps_tr32[:, pb, 0:128],
                  (RPT,), (c.reg(c.zf, ci, (j * J32, (j + 1) * J32)),))


def phase_l0_weights(P, c, i):
    RW = c.reg(c.wqk)
    for ci in range(NCH):
        dma(P, "gpsimd", c.wqk[:, ci, 0:ADIM],
            c.w_l0_Wq[i, ci * 128:(ci + 1) * 128, :], (), (RW,))
        dma(P, "gpsimd", c.wqk[:, ci, ADIM:2 * ADIM],
            c.w_l0_Wk[i, ci * 128:(ci + 1) * 128, :], (), (RW,))
    for kc in range(2):
        dma(P, "gpsimd", c.wpf[:, kc, :],
            c.w_l0_Wp[i, kc * 128:(kc + 1) * 128, :], (), (c.reg(c.wpf),))
        dma(P, "gpsimd", c.wpf2[:, kc, :],
            c.w_l0_Wf[i, kc * 128:(kc + 1) * 128, :], (), (c.reg(c.wpf2),))


def phase_l0_layer(P, c, i):
    RSQ = c.reg(c.sq)
    RSC0 = c.reg(c.sc, (0, 512))
    R_ROW = c.reg(c.sc, (1024, 1536))
    lr = c.sc[0:1, 1024:1536]
    for j in range(NJ8):
        js = (j * J8, (j + 1) * J8)
        jss = slice(*js)
        pj = j % 2

        def srcb(ci, _j=j):
            pz = ci % 2
            apc = c.zbt[:, pz, :]
            key = c.reg(c.zbt, pz)
            act(P, apc, c.zf[:, ci, _j * J8:(_j + 1) * J8], AF.Copy,
                (rZ(c, ci, _j),), (key,))
            return apc, [key]

        emit_seg_stats(P, c, j, srcb, NCH, EPS_ILN, D)
        for ci in range(NCH):
            emit_ln_apply(P, c, j, ci, c.ht[:, 0, ci, :], c.reg(c.ht, 0, ci),
                          "l0_ln1_g", "l0_ln1_b", i)
        for s in range(2):
            for oc in range(2):
                bank = oc
                RPM = c.regp(c.ps_main, bank)
                for ci in range(NCH):
                    mm(P, c.ps_main[:, bank, :],
                       c.wqk[:, ci, s * ADIM + oc * 128:s * ADIM + (oc + 1) * 128],
                       c.ht[:, 0, ci, :],
                       (c.reg(c.wqk), c.reg(c.ht, 0, ci)), RPM,
                       start=(ci == 0), stop=(ci == NCH - 1))
                vts(P, c.qk_sc[:, s, oc, :], c.ps_main[:, bank, :],
                    c.ccol("l0_bqk", i, s * 2 + oc), None, ALU.add, ALU.add,
                    (RPM, c.r_const), (c.reg(c.qk_sc, s, oc),))
            RPM2 = c.regp(c.ps_main, 2)
            for oc in range(2):
                act(P, c.sq[:, :], c.qk_sc[:, s, oc, :], AF.Square,
                    (c.reg(c.qk_sc, s, oc),), (RSQ,))
                mm(P, c.ps_main[0:1, 2, :], c.onescol[:, 0:1], c.sq[:, :],
                   (RSQ, c.reg(c.onescol)), RPM2, start=(oc == 0), stop=(oc == 1),
                   solo=True)
            act(P, lr, c.ps_main[0:1, 2, :], AF.Sqrt, (RPM2,), (R_ROW,))
            vts(P, lr, lr, EPS_NRM, None, ALU.max, ALU.max, (R_ROW,), (R_ROW,))
            vrecip(P, lr, lr, (R_ROW,), (R_ROW,))
            act(P, c.rinvrow[0:1, :], lr, AF.Copy, (R_ROW,), (c.reg(c.rinvrow),))
            emit_bcast(P, c, c.rinvrow[0:1, :], c.reg(c.rinvrow),
                       c.ps_main[:, 2, :], RPM2)
            for oc in range(2):
                ch = s * 2 + oc
                vtt(P, c.qkb[:, ch, jss], c.qk_sc[:, s, oc, :],
                    c.ps_main[:, 2, :], ALU.mult,
                    (c.reg(c.qk_sc, s, oc), RPM2), (c.reg(c.qkb, ch, js),))
        RPM3 = c.regp(c.ps_main, 3)
        for oc in range(2):
            mm(P, c.ps_main[0:1, 3, :], c.wgcol[:, i, oc:oc + 1],
               c.qkb[:, oc, jss],
               (c.reg(c.wgcol), c.reg(c.qkb, oc, js)), RPM3,
               start=(oc == 0), stop=(oc == 1))
        R1 = c.reg(c.st_one)
        vts(P, lr, c.ps_main[0:1, 3, :], SCALE_A, None, ALU.mult, ALU.mult,
            (RPM3,), (R_ROW,))
        act(P, c.aseg[0:1, :], lr, AF.Copy, (R_ROW,), (c.reg(c.aseg),))
        act(P, lr, lr, AF.Square, (R_ROW,), (R_ROW,))
        vred(P, c.st_one[0:1, 2 + j:3 + j], lr, ALU.add, (R_ROW,), (R1,))
        dma(P, "sync", c.arow_scr[0:1, jss], c.aseg[0:1, :],
            (c.reg(c.aseg),), (("H:arow_scr", j * J8 * 2, (j + 1) * J8 * 2),))

    R1 = c.reg(c.st_one)
    vred(P, c.st_one[0:1, 0:1], c.st_one[0:1, 2:10], ALU.add, (R1,), (R1,))
    act(P, c.st_one[0:1, 0:1], c.st_one[0:1, 0:1], AF.Sqrt, (R1,), (R1,))
    vts(P, c.st_one[0:1, 0:1], c.st_one[0:1, 0:1], EPS_NRM, None, ALU.max,
        ALU.max, (R1,), (R1,))
    vrecip(P, c.st_one[0:1, 1:2], c.st_one[0:1, 0:1], (R1,), (R1,))
    for j in range(NJ8):
        js = (j * J8, (j + 1) * J8)
        jss = slice(*js)
        dma(P, "sync", c.aseg[0:1, :], c.arow_scr[0:1, jss],
            (("H:arow_scr", j * J8 * 2, (j + 1) * J8 * 2),), (c.reg(c.aseg),))
        vsts(P, c.aseg[0:1, :], c.aseg[0:1, :], c.st_one[0:1, 1:2], ALU.mult,
             (c.reg(c.aseg), R1), (c.reg(c.aseg),))
        RPM2 = c.regp(c.ps_main, 2)
        emit_bcast(P, c, c.aseg[0:1, :], c.reg(c.aseg), c.ps_main[:, 2, :], RPM2)
        for oc in range(2):
            vtt(P, c.sq[:, :], c.qkb[:, oc, jss], c.ps_main[:, 2, :], ALU.mult,
                (c.reg(c.qkb, oc, js), RPM2), (RSQ,))
            vred(P, c.gpart[:, oc, j:j + 1], c.sq[:, :], ALU.add, (RSQ,),
                 (c.reg(c.gpart, oc),))
    for oc in range(2):
        vred(P, c.gcol[:, oc:oc + 1], c.gpart[:, oc, :], ALU.add,
             (c.reg(c.gpart, oc),), (c.reg(c.gcol),))

    for j in range(NJ8):
        js = (j * J8, (j + 1) * J8)
        jss = slice(*js)
        for oc in range(2):
            vts(P, c.kg[:, oc, :], c.qkb[:, 2 + oc, jss], c.gcol[:, oc:oc + 1],
                None, ALU.mult, ALU.mult,
                (c.reg(c.qkb, 2 + oc, js), c.reg(c.gcol)), (c.reg(c.kg, oc),))
        for oc in range(2):
            bank = oc
            RPM = c.regp(c.ps_main, bank)
            for kc in range(2):
                mm(P, c.ps_main[:, bank, :],
                   c.wpf[:, kc, oc * 128:(oc + 1) * 128], c.kg[:, kc, :],
                   (c.reg(c.wpf), c.reg(c.kg, kc)), RPM,
                   start=(kc == 0), stop=(kc == 1))
            vts(P, c.sc[:, 0:512], c.ps_main[:, bank, :],
                c.ccol("l0_bp", i, oc), None, ALU.add, ALU.add,
                (RPM, c.r_const), (RSC0,))
            vtt(P, c.outt[:, oc, :], c.sc[:, 0:512], c.qkb[:, oc, jss], ALU.add,
                (RSC0, c.reg(c.qkb, oc, js)), (c.reg(c.outt, oc),))
        for ocd in range(NCH):
            bank = 2 + (ocd % 2)
            RPM = c.regp(c.ps_main, bank)
            for kc in range(2):
                mm(P, c.ps_main[:, bank, :],
                   c.wpf2[:, kc, ocd * 128:(ocd + 1) * 128], c.outt[:, kc, :],
                   (c.reg(c.wpf2), c.reg(c.outt, kc)), RPM,
                   start=(kc == 0), stop=(kc == 1))
            vstt(P, c.zf[:, ocd, jss], c.ps_main[:, bank, :],
                 c.ccol("l0_bf", i, ocd), c.zf[:, ocd, jss], ALU.add, ALU.add,
                 (RPM, rZ(c, ocd, j), c.r_const), (rZ(c, ocd, j),))

    nog = HID0 // 512
    for half in range(2):
        jlo = half * (NJ8 // 2)
        for j in range(jlo, jlo + NJ8 // 2):
            jr = j - jlo
            pj = j % 2

            def srcb(ci, _j=j):
                pz = ci % 2
                apc = c.zbt[:, pz, :]
                key = c.reg(c.zbt, pz)
                act(P, apc, c.zf[:, ci, _j * J8:(_j + 1) * J8], AF.Copy,
                    (rZ(c, ci, _j),), (key,))
                return apc, [key]

            emit_seg_stats(P, c, j, srcb, NCH, EPS_ILN, D)
            for ci in range(NCH):
                emit_ln_apply(P, c, j, ci,
                              c.h2b[:, ci, jr * J8:(jr + 1) * J8],
                              c.reg(c.h2b, ci, (jr * J8, (jr + 1) * J8)),
                              "l0_ln2_g", "l0_ln2_b", i)
        for og in range(nog):
            RW1 = c.reg(c.w1g)
            RW2 = c.reg(c.w2g)
            for ci in range(NCH):
                dma(P, "gpsimd", c.w1g[:, ci, :],
                    c.w_l0_W1[i, ci * 128:(ci + 1) * 128, og * 512:(og + 1) * 512],
                    (), (RW1,))
            for kc in range(4):
                dma(P, "gpsimd", c.w2g[:, kc, :],
                    c.w_l0_W2[i, og * 512 + kc * 128:og * 512 + (kc + 1) * 128, :],
                    (), (RW2,))
            for j in range(jlo, jlo + NJ8 // 2):
                js = (j * J8, (j + 1) * J8)
                jss = slice(*js)
                jr = j - jlo
                for oc4 in range(4):
                    bank = oc4 % 2
                    RPM = c.regp(c.ps_main, bank)
                    for ci in range(NCH):
                        mm(P, c.ps_main[:, bank, :],
                           c.w1g[:, ci, oc4 * 128:(oc4 + 1) * 128],
                           c.h2b[:, ci, jr * J8:(jr + 1) * J8],
                           (RW1, c.reg(c.h2b, ci, (jr * J8, (jr + 1) * J8))),
                           RPM, start=(ci == 0), stop=(ci == NCH - 1))
                    act(P, c.f1[:, oc4, :], c.ps_main[:, bank, :], AF.Gelu,
                        (RPM, c.r_const), (c.reg(c.f1, oc4),),
                        bias=c.ccol("l0_b1", i, og * 4 + oc4))
                for ocd in range(NCH):
                    bank = 2 + (ocd % 2)
                    RPM = c.regp(c.ps_main, bank)
                    for kc in range(4):
                        mm(P, c.ps_main[:, bank, :],
                           c.w2g[:, kc, ocd * 128:(ocd + 1) * 128],
                           c.f1[:, kc, :], (RW2, c.reg(c.f1, kc)), RPM,
                           start=(kc == 0), stop=(kc == 3))
                    if og == nog - 1:
                        vstt(P, c.zf[:, ocd, jss], c.ps_main[:, bank, :],
                             c.ccol("l0_b2", i, ocd), c.zf[:, ocd, jss],
                             ALU.add, ALU.add,
                             (RPM, rZ(c, ocd, j), c.r_const), (rZ(c, ocd, j),))
                    else:
                        vtt(P, c.zf[:, ocd, jss], c.ps_main[:, bank, :],
                            c.zf[:, ocd, jss], ALU.add,
                            (RPM, rZ(c, ocd, j)), (rZ(c, ocd, j),))



def phase_l0_ffn(P, c):
    RRW = c.reg(c.rwT)
    for ci in range(NCH):
        dma(P, "gpsimd", c.rwT[:, ci, :],
            c.w_f0_rwT[ci * 128:(ci + 1) * 128, :], (), (RRW,))
    for half in range(2):
        h0 = half * (T // 2)
        for jj in range(16):
            j = half * 16 + jj
            tsl = slice(j * J32, (j + 1) * J32)
            hsl = slice(jj * J32, (jj + 1) * J32)
            RT = c.reg(c.lt, 0)
            RB2 = c.reg(c.ltb2)
            RSQ2 = c.reg(c.ltsq2)
            RSTT = c.reg(c.ltstat2)
            dma(P, "sync", c.lt[:, 0, :], c.x_in[tsl, :], (), (RT,))
            vred(P, c.ltstat2[:, 0:1], c.lt[:, 0, :], ALU.add, (RT,), (RSTT,))
            act(P, c.ltsq2[:, :], c.lt[:, 0, :], AF.Square, (RT,), (RSQ2,))
            vred(P, c.ltstat2[:, 1:2], c.ltsq2[:, :], ALU.add, (RSQ2,), (RSTT,))
            vts(P, c.ltstat2[:, 0:1], c.ltstat2[:, 0:1], 1.0 / D, None,
                ALU.mult, ALU.mult, (RSTT,), (RSTT,))
            vts(P, c.ltstat2[:, 1:2], c.ltstat2[:, 1:2], 1.0 / D, None,
                ALU.mult, ALU.mult, (RSTT,), (RSTT,))
            vtt(P, c.ltstat2[:, 2:3], c.ltstat2[:, 0:1], c.ltstat2[:, 0:1],
                ALU.mult, (RSTT,), (RSTT,))
            vtt(P, c.ltstat2[:, 2:3], c.ltstat2[:, 1:2], c.ltstat2[:, 2:3],
                ALU.subtract, (RSTT,), (RSTT,))
            vts(P, c.ltstat2[:, 2:3], c.ltstat2[:, 2:3], EPS_OLN, None,
                ALU.add, ALU.add, (RSTT,), (RSTT,))
            act(P, c.ltstat2[:, 2:3], c.ltstat2[:, 2:3], AF.Sqrt, (RSTT,), (RSTT,))
            vrecip(P, c.ltstat2[:, 3:4], c.ltstat2[:, 2:3], (RSTT,), (RSTT,))
            vts(P, c.lt[:, 0, :], c.lt[:, 0, :], c.ltstat2[:, 0:1],
                c.ltstat2[:, 3:4], ALU.subtract, ALU.mult, (RT, RSTT), (RT,))
            act(P, c.ltb2[:, :], c.lt[:, 0, :], AF.Copy, (RT,), (RB2,))
            for ci in range(NCH):
                RPT = c.regp(c.ps_trb, 0)
                transp(P, c.ps_trb[:, 0, 0:128], c.ltb2[:, ci * 128:(ci + 1) * 128],
                       c.ident[:, :], (RB2, c.reg(c.ident)), RPT)
                vcopy(P, c.xnh[:, ci, hsl], c.ps_trb[:, 0, 0:128], (RPT,),
                      (c.reg(c.xnh, ci, (jj * J32, (jj + 1) * J32)),))
            RPA0 = c.regp(c.ps_aux, 0)
            for ci in range(NCH):
                mm(P, c.ps_aux[:, 0, 0:8], c.xnh[:, ci, hsl], c.rwT[:, ci, :],
                   (c.reg(c.xnh, ci, (jj * J32, (jj + 1) * J32)), RRW), RPA0,
                   start=(ci == 0), stop=(ci == NCH - 1), solo=True)
            RRT = c.reg(c.rt)
            lg = c.rt[:, 0:8]
            vcopy(P, lg, c.ps_aux[:, 0, 0:8], (RPA0,), (RRT,))
            m1 = c.rt[:, 8:9]
            m2 = c.rt[:, 9:10]
            msk = c.rt[:, 10:18]
            rden = c.rt[:, 11:12]
            rden = c.rt[:, 18:19]
            negm1 = c.rt[:, 19:20]
            eterm = c.rt[:, 20:28]
            vred(P, m1, lg, ALU.max, (RRT,), (RRT,))
            vts(P, msk, lg, m1, None, ALU.is_equal, ALU.is_equal, (RRT,), (RRT,))
            vstt(P, msk, msk, -1e30, lg, ALU.mult, ALU.add, (RRT,), (RRT,))
            vred(P, m2, msk, ALU.max, (RRT,), (RRT,))
            vtt(P, c.rt[:, 28:29], m2, m1, ALU.subtract, (RRT,), (RRT,))
            act(P, c.rt[:, 28:29], c.rt[:, 28:29], AF.Exp, (RRT,), (RRT,))
            vts(P, c.rt[:, 28:29], c.rt[:, 28:29], 1.0, None, ALU.add, ALU.add,
                (RRT,), (RRT,))
            vrecip(P, rden, c.rt[:, 28:29], (RRT,), (RRT,))
            vts(P, negm1, m1, -1.0, None, ALU.mult, ALU.mult, (RRT,), (RRT,))
            act(P, eterm, lg, AF.Exp, (RRT,), (RRT,), bias=negm1)
            vts(P, c.rt[:, 10:18], lg, m2, None, ALU.is_ge, ALU.is_ge,
                (RRT,), (RRT,))
            vstt(P, eterm, eterm, rden, c.rt[:, 10:18], ALU.mult, ALU.mult,
                 (RRT,), (RRT,))
            act(P, eterm, eterm, AF.Sqrt, (RRT,), (RRT,))
            RPT2 = c.regp(c.ps_tr32, 1)
            transp(P, c.ps_tr32[0:8, 1, 0:128], eterm, c.ident32[:, :],
                   (RRT, c.reg(c.ident32)), RPT2)
            vcopy(P, c.swem[0:8, hsl], c.ps_tr32[0:8, 1, 0:128], (RPT2,),
                  (c.reg(c.swem),))
        for e in range(8):
            pe_ = e % 2
            RW1 = c.reg(c.w1e, pe_)
            RW2 = c.reg(c.w2e)
            if half == 0 and e == 0:
                for ci in range(NCH):
                    dma(P, "gpsimd", c.w1e[:, 0, ci, :],
                        c.w_f0_w1[0, ci * 128:(ci + 1) * 128, :], (), (c.reg(c.w1e, 0),))
            if e < 7:
                for ci in range(NCH):
                    dma(P, "gpsimd", c.w1e[:, (e + 1) % 2, ci, :],
                        c.w_f0_w1[e + 1, ci * 128:(ci + 1) * 128, :], (),
                        (c.reg(c.w1e, (e + 1) % 2),))
            elif half == 0:
                for ci in range(NCH):
                    dma(P, "gpsimd", c.w1e[:, 0, ci, :],
                        c.w_f0_w1[0, ci * 128:(ci + 1) * 128, :], (),
                        (c.reg(c.w1e, 0),))
            RSW = c.reg(c.swrow)
            dma(P, "sync", c.swrow[0:1, :], c.swem[e:e + 1, :],
                (c.reg(c.swem),), (RSW,))
            for jj4 in range(4):
                js_h = slice(jj4 * J8, (jj4 + 1) * J8)
                jg = half * (T // 2) + jj4 * J8
                pf = jj4 % 2
                RPA1 = c.regp(c.ps_aux, 1)
                emit_bcast(P, c, c.swrow[0:1, js_h], RSW, c.ps_aux[:, 1, :], RPA1)
                for hc in range(16):
                    bank = hc % 2
                    RPM = c.regp(c.ps_main, bank)
                    for ci in range(NCH):
                        mm(P, c.ps_main[:, bank, :],
                           c.w1e[:, pe_, ci, hc * 128:(hc + 1) * 128],
                           c.xnh[:, ci, js_h],
                           (RW1, c.reg(c.xnh, ci, (jj4 * J8, (jj4 + 1) * J8))),
                           RPM, start=(ci == 0), stop=(ci == NCH - 1))
                    RFS = c.reg(c.ffnsc, hc % 2)
                    vts(P, c.ffnsc[:, hc % 2, :], c.ps_main[:, bank, :], 0.0,
                        None, ALU.max, ALU.max, (RPM,), (RFS,))
                    vtt(P, c.ffnsc[:, hc % 2, :], c.ffnsc[:, hc % 2, :],
                        c.ps_aux[:, 1, :], ALU.mult, (RFS, RPA1), (RFS,))
                    act(P, c.f1e[:, pf, hc, :], c.ffnsc[:, hc % 2, :], AF.Square,
                        (RFS,), (c.reg(c.f1e, pf, hc),))
                if jj4 == 0:
                    for kc in range(16):
                        dma(P, "gpsimd", c.w2e[:, kc, :],
                            c.w_f0_w2[e, kc * 128:(kc + 1) * 128, :], (), (RW2,))
                for oc in range(NCH):
                    bank = 2 + (oc % 2)
                    RPM = c.regp(c.ps_main, bank)
                    for kc in range(16):
                        mm(P, c.ps_main[:, bank, :],
                           c.w2e[:, kc, oc * 128:(oc + 1) * 128],
                           c.f1e[:, pf, kc, :],
                           (RW2, c.reg(c.f1e, pf, kc)), RPM,
                           start=(kc == 0), stop=(kc == 15))
                    ps_ = (oc // 4) % 2
                    vcopy(P, c.stg[:, ps_, oc % 4, :], c.ps_main[:, bank, :],
                          (RPM,), (c.reg(c.stg, ps_, oc % 4),))
                    if oc % 4 == 3:
                        o0 = (oc // 4) * 4
                        dma(P, "gpsimd",
                            c.acc_fm.rearrange("(o p) t -> p o t", p=128)[
                                :, o0:o0 + 4, jg:jg + J8],
                            c.stg[:, ps_, :, :],
                            (c.reg(c.stg, ps_),),
                            (hreg2("acc_fm", T, 4, o0 * 128, (o0 + 4) * 128,
                                   jg, jg + J8),), accum=True)

def phase_spill_z4(P, c):
    for ci in range(NCH):
        for j in range(NJ8):
            jss = slice(j * J8, (j + 1) * J8)
            dma(P, "sync", c.acc_fm[ci * 128:(ci + 1) * 128, jss],
                c.zf[:, ci, jss], (rZ(c, ci, j),),
                (hreg2("acc_fm", T, 4, ci * 128, (ci + 1) * 128,
                       j * J8, (j + 1) * J8),))



def top2_weights(P, c, lg_psum_rd, rt, RRT, sqrt_w):
    lg = rt[:, 0:8]
    m1 = rt[:, 8:9]
    m2 = rt[:, 9:10]
    msk = rt[:, 10:18]
    rden = rt[:, 18:19]
    negm1 = rt[:, 19:20]
    eterm = rt[:, 20:28]
    tmp = rt[:, 28:29]
    vred(P, m1, lg, ALU.max, (RRT,), (RRT,))
    vts(P, msk, lg, m1, None, ALU.is_equal, ALU.is_equal, (RRT,), (RRT,))
    vstt(P, msk, msk, -1e30, lg, ALU.mult, ALU.add, (RRT,), (RRT,))
    vred(P, m2, msk, ALU.max, (RRT,), (RRT,))
    vtt(P, tmp, m2, m1, ALU.subtract, (RRT,), (RRT,))
    act(P, tmp, tmp, AF.Exp, (RRT,), (RRT,))
    vts(P, tmp, tmp, 1.0, None, ALU.add, ALU.add, (RRT,), (RRT,))
    vrecip(P, rden, tmp, (RRT,), (RRT,))
    vts(P, negm1, m1, -1.0, None, ALU.mult, ALU.mult, (RRT,), (RRT,))
    act(P, eterm, lg, AF.Exp, (RRT,), (RRT,), bias=negm1)
    vts(P, msk, lg, m2, None, ALU.is_ge, ALU.is_ge, (RRT,), (RRT,))
    vstt(P, eterm, eterm, rden, msk, ALU.mult, ALU.mult, (RRT,), (RRT,))
    if sqrt_w:
        act(P, eterm, eterm, AF.Sqrt, (RRT,), (RRT,))


def phase_l1_attn(P, c):
    accfm_v = c.acc_fm.rearrange("(c p) t -> p c t", p=128)
    RWQK = c.reg(c.wqk1)
    for s, w_in in ((0, c.w_l1_Wq), (1, c.w_l1_Wk)):
        for ci in range(NCH):
            dma(P, "gpsimd", c.wqk1[:, s, ci, :],
                w_in[ci * 128:(ci + 1) * 128, :], (), (RWQK,))
    RCS0 = c.regp(c.ps_aux, 0)
    RCS1 = c.regp(c.ps_aux, 1)
    for j in range(NJ32):
        tsl = slice(j * J32, (j + 1) * J32)
        RT = c.reg(c.lt1)
        RAF = c.reg(c.af1)
        RB = c.reg(c.ltb1)
        RSQ = c.reg(c.ltsq1)
        RSTT = c.reg(c.lstat1)
        dma(P, "sync", c.lt1[:, :], c.acc_tm[tsl, :],
            (hreg2("acc_tm", D, 4, j * J32, (j + 1) * J32, 0, D),), (RT,))
        dma(P, "sync", c.af1[:, :, :], accfm_v[:, :, tsl],
            (hreg2("acc_fm", T, 4, 0, D, j * J32, (j + 1) * J32),), (RAF,))
        for ci in range(NCH):
            pb = ci % 2
            RPT = c.regp(c.ps_tr32, pb)
            transp(P, c.ps_tr32[:, pb, 0:128], c.af1[:, ci, :], c.ident32[:, :],
                   (RAF, c.reg(c.ident32)), RPT)
            vtt(P, c.lt1[:, ci * 128:(ci + 1) * 128],
                c.lt1[:, ci * 128:(ci + 1) * 128], c.ps_tr32[:, pb, 0:128],
                ALU.add, (RT, RPT), (RT,))
        vred(P, c.lstat1[:, 0:1], c.lt1[:, :], ALU.add, (RT,), (RSTT,))
        act(P, c.ltsq1[:, :], c.lt1[:, :], AF.Square, (RT,), (RSQ,))
        vred(P, c.lstat1[:, 1:2], c.ltsq1[:, :], ALU.add, (RSQ,), (RSTT,))
        vts(P, c.lstat1[:, 0:1], c.lstat1[:, 0:1], 1.0 / D, None, ALU.mult,
            ALU.mult, (RSTT,), (RSTT,))
        vts(P, c.lstat1[:, 1:2], c.lstat1[:, 1:2], 1.0 / D, None, ALU.mult,
            ALU.mult, (RSTT,), (RSTT,))
        vtt(P, c.lstat1[:, 2:3], c.lstat1[:, 0:1], c.lstat1[:, 0:1], ALU.mult,
            (RSTT,), (RSTT,))
        vtt(P, c.lstat1[:, 2:3], c.lstat1[:, 1:2], c.lstat1[:, 2:3],
            ALU.subtract, (RSTT,), (RSTT,))
        vts(P, c.lstat1[:, 2:3], c.lstat1[:, 2:3], EPS_OLN, None, ALU.add,
            ALU.add, (RSTT,), (RSTT,))
        act(P, c.lstat1[:, 2:3], c.lstat1[:, 2:3], AF.Sqrt, (RSTT,), (RSTT,))
        vrecip(P, c.lstat1[:, 3:4], c.lstat1[:, 2:3], (RSTT,), (RSTT,))
        vts(P, c.lt1[:, :], c.lt1[:, :], c.lstat1[:, 0:1], c.lstat1[:, 3:4],
            ALU.subtract, ALU.mult, (RT, RSTT), (RT,))
        act(P, c.ltb1[:, :], c.lt1[:, :], AF.Copy, (RT,), (RB,))
        RXT = c.reg(c.xn_tm, j)
        vcopy(P, c.xn_tm[:, j, :], c.ltb1[:, :], (RB,), (RXT,), eng="gpsimd")
        mm(P, c.ps_aux[0:1, 0, :], c.onescol[:, 0:1], c.ltb1[:, 0:512],
           (RB, c.reg(c.onescol)), RCS0, start=(j == 0), stop=(j == NJ32 - 1),
           solo=True)
        mm(P, c.ps_aux[0:1, 1, :], c.onescol[:, 0:1], c.ltb1[:, 512:1024],
           (RB, c.reg(c.onescol)), RCS1, start=(j == 0), stop=(j == NJ32 - 1),
           solo=True)
        jr = (j % 4) * J32
        pj8 = (j // 4) % 2
        RXF = c.reg(c.xnfm_t, pj8)
        for ci in range(NCH):
            RPT = c.regp(c.ps_trb, ci % 2)
            transp(P, c.ps_trb[:, ci % 2, 0:128],
                   c.ltb1[:, ci * 128:(ci + 1) * 128], c.ident[:, :],
                   (RB, c.reg(c.ident)), RPT)
            vcopy(P, c.xnfm_t[:, pj8, ci, jr:jr + J32],
                  c.ps_trb[:, ci % 2, 0:128], (RPT,), (RXF,))
        if j % 4 == 3:
            jj8 = j // 4
            g0 = jj8 * J8
            for ci in range(NCH):
                dma(P, "sync", c.xnfm_scr[ci * 128:(ci + 1) * 128, g0:g0 + J8],
                    c.xnfm_t[:, pj8, ci, :], (RXF,),
                    (hreg2("xnfm_scr", T, 2, ci * 128, (ci + 1) * 128,
                           g0, g0 + J8),))
            for s in range(2):
                for oc in range(NCH):
                    bank = oc % 2
                    RPM = c.regp(c.ps_main, bank)
                    for ci in range(NCH):
                        mm(P, c.ps_main[:, bank, :],
                           c.wqk1[:, s, ci, oc * 128:(oc + 1) * 128],
                           c.xnfm_t[:, pj8, ci, :], (RWQK, RXF), RPM,
                           start=(ci == 0), stop=(ci == NCH - 1))
                    bcol = c.ccol("l1_bq" if s == 0 else "l1_bk", 0, oc)
                    if s == 0:
                        ps_ = oc % 2
                        RSTQ = c.reg(c.stq, ps_)
                        vts(P, c.stq[:, ps_, 0:512], c.ps_main[:, bank, :],
                            bcol, None, ALU.add, ALU.add,
                            (RPM, c.r_const), (RSTQ,))
                        dma(P, "sync",
                            c.q_scr[oc * 128:(oc + 1) * 128, g0:g0 + J8],
                            c.stq[:, ps_, 0:512], (RSTQ,),
                            (hreg2("q_scr", T, 2, oc * 128, (oc + 1) * 128,
                                   g0, g0 + J8),))
                    else:
                        vts(P, c.k_em[:, oc, g0:g0 + J8], c.ps_main[:, bank, :],
                            bcol, None, ALU.add, ALU.add, (RPM, c.r_const),
                            (c.reg(c.k_em, oc, (g0, g0 + J8)),))
    RCR = c.reg(c.csum_r)
    RCB = c.reg(c.csum_b)
    vcopy(P, c.csum_r[0:1, 0:512], c.ps_aux[0:1, 0, :], (RCS0,), (RCR,))
    vcopy(P, c.csum_r[0:1, 512:1024], c.ps_aux[0:1, 1, :], (RCS1,), (RCR,))
    act(P, c.csum_b[0:1, :], c.csum_r[0:1, :], AF.Copy, (RCR,), (RCB,))
    for hh in range(2):
        RPA = c.regp(c.ps_aux, hh)
        emit_bcast(P, c, c.csum_b[0:1, hh * 512:(hh + 1) * 512], RCB,
                   c.ps_aux[:, hh, :], RPA)
        vcopy(P, c.csb[:, hh * 512:(hh + 1) * 512], c.ps_aux[:, hh, :],
              (RPA,), (c.reg(c.csb),))

    RS = c.reg(c.S)
    REB = c.reg(c.Eb)
    RA = c.reg(c.acol)
    for sb in range(NJ32):
        s0 = sb * J32
        pq = sb % 2
        RQ = c.reg(c.qsl, pq)
        dma(P, "sync", c.qsl[:, pq, :, :],
            c.q_scr.rearrange("(c p) t -> p c t", p=128)[:, :, s0:s0 + J32],
            (hreg2("q_scr", T, 2, 0, D, s0, s0 + J32),), (RQ,))
        for tc in range(NJ8):
            bank = 2 + tc % 2
            RPM = c.regp(c.ps_main, bank)
            for ci in range(NCH):
                mm(P, c.ps_main[:, bank, :], c.qsl[:, pq, ci, :],
                   c.k_em[:, ci, tc * J8:(tc + 1) * J8],
                   (RQ, c.reg(c.k_em, ci, (tc * J8, (tc + 1) * J8))), RPM,
                   start=(ci == 0), stop=(ci == NCH - 1))
            vts(P, c.S[:, tc * J8:(tc + 1) * J8], c.ps_main[:, bank, :],
                INVSQD, None, ALU.mult, ALU.mult, (RPM,), (RS,))
        vred(P, c.acol[:, 0:1], c.S[:, :], ALU.max, (RS,), (RA,))
        vts(P, c.acol[:, 0:1], c.acol[:, 0:1], -1.0, None, ALU.mult, ALU.mult,
            (RA,), (RA,))
        act(P, c.Eb[:, :], c.S[:, :], AF.Exp, (RS, RA), (REB, RA),
            bias=c.acol[:, 0:1], accum=c.acol[:, 1:2])
        vrecip(P, c.acol[:, 2:3], c.acol[:, 1:2], (RA,), (RA,))
        vtt(P, c.acol[:, 3:4], c.acol[:, 1:2], c.acol[:, 2:3], ALU.mult,
            (RA,), (RA,))
        vts(P, c.acol[:, 3:4], c.acol[:, 3:4], 1.0, None, ALU.add, ALU.add,
            (RA,), (RA,))
        vts(P, c.acol[:, 3:4], c.acol[:, 3:4], -1.0 / T, None, ALU.mult,
            ALU.mult, (RA,), (RA,))
        RPM0 = c.regp(c.ps_main, 0)
        RPM1 = c.regp(c.ps_main, 1)
        for tc in range(NJ32):
            pb = tc % 2
            RPT = c.regp(c.ps_trb, pb)
            RET = c.reg(c.ebt, pb)
            transp(P, c.ps_trb[:, pb, 0:128], c.Eb[:, tc * J32:(tc + 1) * J32],
                   c.ident[:, :], (REB, c.reg(c.ident)), RPT)
            vcopy(P, c.ebt[:, pb, :], c.ps_trb[:, pb, 0:128], (RPT,), (RET,))
            mm(P, c.ps_main[:, 0, :], c.ebt[:, pb, :], c.xn_tm[:, tc, 0:512],
               (RET, c.reg(c.xn_tm, tc)), RPM0,
               start=(tc == 0), stop=(tc == NJ32 - 1), solo=True)
            mm(P, c.ps_main[:, 1, :], c.ebt[:, pb, :], c.xn_tm[:, tc, 512:1024],
               (RET, c.reg(c.xn_tm, tc)), RPM1,
               start=(tc == 0), stop=(tc == NJ32 - 1), solo=True)
        ps_ = sb % 2
        RSTA = c.reg(c.sta, ps_)
        for hh in range(2):
            RPMh = c.regp(c.ps_main, hh)
            hs = slice(hh * 512, (hh + 1) * 512)
            vts(P, c.sta[:, ps_, hs], c.ps_main[:, hh, :], c.acol[:, 2:3],
                None, ALU.mult, ALU.mult, (RPMh, RA), (RSTA,))
            vstt(P, c.sta[:, ps_, hs], c.csb[:, hs], c.acol[:, 3:4],
                 c.sta[:, ps_, hs], ALU.mult, ALU.add,
                 (c.reg(c.csb), RA, RSTA), (RSTA,))
            vtt(P, c.sta[:, ps_, hs], c.sta[:, ps_, hs], c.xn_tm[:, sb, hs],
                ALU.add, (RSTA, c.reg(c.xn_tm, sb)), (RSTA,))
        dma(P, "gpsimd", c.acc_tm[s0:s0 + J32, :], c.sta[:, ps_, :],
            (RSTA,), (hreg2("acc_tm", D, 4, s0, s0 + J32, 0, D),), accum=True)


def phase_l1_ffn(P, c):
    xnv = c.xnfm_scr.rearrange("(c p) t -> p c t", p=128)
    RRW = c.reg(c.rwT1)
    for ci in range(NCH):
        dma(P, "gpsimd", c.rwT1[:, ci, :],
            c.w_f1_rwT[ci * 128:(ci + 1) * 128, :], (), (RRW,))
    dma(P, "gpsimd", c.rbrow.ap(), c.rb_in, (), (c.reg(c.rbrow),))
    RPAB = c.regp(c.ps_aux, 1)
    emit_bcast(P, c, c.rbrow[0:1, :], c.reg(c.rbrow), c.ps_aux[:, 1, 0:8], RPAB)
    vcopy(P, c.rbb[:, :], c.ps_aux[:, 1, 0:8], (RPAB,), (c.reg(c.rbb),))
    for q in range(4):
        q0 = q * D
        RXQ = c.reg(c.xnq)
        for ci in range(NCH):
            dma(P, "sync", c.xnq[:, ci, :], xnv[:, ci, q0:q0 + D],
                (hreg2("xnfm_scr", T, 2, ci * 128, (ci + 1) * 128, q0, q0 + D),),
                (RXQ,))
        RRT = c.reg(c.rt1)
        RPA0 = c.regp(c.ps_aux, 0)
        RWEM = c.reg(c.wem1)
        for t8 in range(8):
            t0 = t8 * J32
            for ci in range(NCH):
                mm(P, c.ps_aux[:, 0, 0:8], c.xnq[:, ci, t0:t0 + J32],
                   c.rwT1[:, ci, :], (RXQ, RRW), RPA0,
                   start=(ci == 0), stop=(ci == NCH - 1), solo=True)
            vtt(P, c.rt1[:, 0:8], c.ps_aux[:, 0, 0:8], c.rbb[:, :], ALU.add,
                (RPA0, c.reg(c.rbb)), (RRT,))
            top2_weights(P, c, None, c.rt1, RRT, sqrt_w=False)
            RPT2 = c.regp(c.ps_tr32, 1)
            transp(P, c.ps_tr32[0:8, 1, 0:128], c.rt1[:, 20:28], c.ident32[:, :],
                   (RRT, c.reg(c.ident32)), RPT2)
            vcopy(P, c.wem1[0:8, t0:t0 + J32], c.ps_tr32[0:8, 1, 0:128],
                  (RPT2,), (RWEM,))
        for tt2 in range(2):
            ts0 = tt2 * J8
            jg = q0 + ts0
            for oc in range(NCH):
                bank = oc % 2
                RPM = c.regp(c.ps_main, bank)
                mm(P, c.ps_main[:, bank, :], c.spb2[0:8, oc * 128:(oc + 1) * 128],
                   c.wem1[0:8, ts0:ts0 + J8], (c.reg(c.spb2), RWEM), RPM,
                   start=True, stop=True)
                ps_ = oc % 2
                RST = c.reg(c.stg1, ps_)
                vts(P, c.stg1[:, ps_, 0, :], c.ps_main[:, bank, :],
                    c.ccol("l1_sh_b2", 0, oc), None, ALU.add, ALU.add,
                    (RPM, c.r_const), (RST,))
                dma(P, "gpsimd",
                    c.acc_fm.rearrange("(o p) t -> p o t", p=128)[
                        :, oc:oc + 1, jg:jg + J8],
                    c.stg1[:, ps_, 0:1, :], (RST,),
                    (hreg2("acc_fm", T, 4, oc * 128, (oc + 1) * 128,
                           jg, jg + J8),), accum=True)
        for m in range(9):
            RW1 = c.reg(c.w1T)
            RW2 = c.reg(c.w2T)
            for ci in range(NCH):
                for fg in range(4):
                    dma(P, "gpsimd", c.w1T[:, ci, fg * 1024:(fg + 1) * 1024],
                        c.w_f1_w1T[m, ci * 128:(ci + 1) * 128,
                                   fg * 1024:(fg + 1) * 1024], (), (RW1,))
            for kc in range(32):
                dma(P, "gpsimd", c.w2T[:, kc, :],
                    c.w_f1_w2T[m, kc * 128:(kc + 1) * 128, :], (), (RW2,))
            if m > 0:
                RWR = c.reg(c.wrow1)
                dma(P, "sync", c.wrow1[0:1, :], c.wem1[m - 1:m, :],
                    (RWEM,), (RWR,))
            for tt2 in range(2):
                ts0 = tt2 * J8
                jg = q0 + ts0
                if m > 0:
                    RPA1 = c.regp(c.ps_aux, 1)
                    emit_bcast(P, c, c.wrow1[0:1, ts0:ts0 + J8], RWR,
                               c.ps_aux[:, 1, :], RPA1)
                for fc in range(32):
                    bank = fc % 2
                    RPM = c.regp(c.ps_main, bank)
                    for ci in range(NCH):
                        mm(P, c.ps_main[:, bank, :],
                           c.w1T[:, ci, fc * 128:(fc + 1) * 128],
                           c.xnq[:, ci, ts0:ts0 + J8], (RW1, RXQ), RPM,
                           start=(ci == 0), stop=(ci == NCH - 1))
                    bcol = c.ccol("l1_sh_b1" if m == 0 else "l1_sp_b1",
                                  0 if m == 0 else m - 1, fc)
                    RF1 = c.reg(c.F1f, fc)
                    if m == 0:
                        act(P, c.F1f[:, fc, :], c.ps_main[:, bank, :],
                            AF.Gelu_apprx_tanh, (RPM, c.r_const), (RF1,),
                            bias=bcol)
                    else:
                        RFS = c.reg(c.fsc, fc % 2)
                        act(P, c.fsc[:, fc % 2, :], c.ps_main[:, bank, :],
                            AF.Gelu_apprx_tanh, (RPM, c.r_const), (RFS,),
                            bias=bcol)
                        vtt(P, c.F1f[:, fc, :], c.fsc[:, fc % 2, :],
                            c.ps_aux[:, 1, :], ALU.mult, (RFS, RPA1), (RF1,))
                for oc in range(NCH):
                    bank = 2 + oc % 2
                    RPM = c.regp(c.ps_main, bank)
                    for kc in range(32):
                        mm(P, c.ps_main[:, bank, :],
                           c.w2T[:, kc, oc * 128:(oc + 1) * 128],
                           c.F1f[:, kc, :], (RW2, c.reg(c.F1f, kc)), RPM,
                           start=(kc == 0), stop=(kc == 31))
                    ps_ = (oc // 2) % 2
                    oo = oc % 2
                    RST = c.reg(c.stg1, ps_)
                    vcopy(P, c.stg1[:, ps_, oo, :], c.ps_main[:, bank, :],
                          (RPM,), (RST,))
                    if oo == 1:
                        o0 = (oc // 2) * 2
                        dma(P, "gpsimd",
                            c.acc_fm.rearrange("(o p) t -> p o t", p=128)[
                                :, o0:o0 + 2, jg:jg + J8],
                            c.stg1[:, ps_, :, :], (RST,),
                            (hreg2("acc_fm", T, 4, o0 * 128, (o0 + 2) * 128,
                                   jg, jg + J8),), accum=True)


def phase_final(P, c):
    accfm_v = c.acc_fm.rearrange("(c p) t -> p c t", p=128)
    for j in range(NJ32):
        tsl = slice(j * J32, (j + 1) * J32)
        RT = c.reg(c.lt1)
        RAF = c.reg(c.af1)
        dma(P, "sync", c.lt1[:, :], c.acc_tm[tsl, :],
            (hreg2("acc_tm", D, 4, j * J32, (j + 1) * J32, 0, D),), (RT,))
        dma(P, "sync", c.af1[:, :, :], accfm_v[:, :, tsl],
            (hreg2("acc_fm", T, 4, 0, D, j * J32, (j + 1) * J32),), (RAF,))
        for ci in range(NCH):
            pb = ci % 2
            RPT = c.regp(c.ps_tr32, pb)
            transp(P, c.ps_tr32[:, pb, 0:128], c.af1[:, ci, :], c.ident32[:, :],
                   (RAF, c.reg(c.ident32)), RPT)
            vtt(P, c.lt1[:, ci * 128:(ci + 1) * 128],
                c.lt1[:, ci * 128:(ci + 1) * 128], c.ps_tr32[:, pb, 0:128],
                ALU.add, (RT, RPT), (RT,))
        dma(P, "sync", c.out[tsl, :], c.lt1[:, :], (RT,),
            (hreg2("out", D, 4, j * J32, (j + 1) * J32, 0, D),))

def emit_tap_z(P, c, tap):
    ap = c.taps[tap]
    for ci in range(NCH):
        for j in range(NJ8):
            jss = slice(j * J8, (j + 1) * J8)
            dma(P, "sync", ap[ci * 128:(ci + 1) * 128, jss], c.zf[:, ci, jss],
                (rZ(c, ci, j),), ())


def emit_copy_hbm(P, c, dst_ap, src_ap, name_src, nrow, ncol, esz):
    step = max(1, (1 << 20) // (ncol * esz))
    r = 0
    while r < nrow:
        r2 = min(nrow, r + step)
        dma(P, "sync", dst_ap[r:r2, :], src_ap[r:r2, :],
            (hreg2(name_src, ncol, esz, r, r2, 0, ncol),), ())
        r = r2


PHASES_ALL = ("ln0", "l0a", "spill", "l0ffn", "l1attn", "l1ffn", "final")


def build_program(phases=PHASES_ALL, taps=(), acc_kind="Internal",
                  acc_taps=(), n_layers=4, scr_kind="Internal"):
    nc = bass.Bass("TRN2", target_bir_lowering=False, debug=False)
    c = Ctx()
    c.scratch_kind = acc_kind
    c.scr_kind = scr_kind
    build_ctx(nc, c, taps=taps)
    for tname in acc_taps:
        c.taps["acc_" + tname] = nc.dram_tensor(
            "tap_acc_" + tname, [T, D] if tname == "tm" else [D, T], F32,
            kind="ExternalOutput").ap()
    P = Prog()
    stack = contextlib.ExitStack()
    sems = {s: stack.enter_context(nc.semaphore(s)) for s in ALLSEMS}
    phase_load_consts(P, c)
    if "ln0" in phases:
        phase_load_ln0(P, c)
        if "xn" in c.taps:
            emit_tap_z(P, c, "xn")
    if "l0a" in phases:
        phase_l0_weights(P, c, 0)
        for i in range(n_layers):
            phase_l0_layer(P, c, i)
            if i + 1 < n_layers:
                phase_l0_weights(P, c, i + 1)
            tp = "z%d" % (i + 1)
            if tp in c.taps:
                emit_tap_z(P, c, tp)
    if "spill" in phases:
        phase_spill_z4(P, c)
    if "l0ffn" in phases:
        phase_l0_ffn(P, c)
    if "l1attn" in phases:
        phase_l1_attn(P, c)
    if "l1ffn" in phases:
        phase_l1_ffn(P, c)
    if "final" in phases:
        phase_final(P, c)
    if "acc_tm" in c.taps:
        emit_copy_hbm(P, c, c.taps["acc_tm"], c.acc_tm, "acc_tm", T, D, 4)
    if "acc_fm" in c.taps:
        emit_copy_hbm(P, c, c.taps["acc_fm"], c.acc_fm, "acc_fm", D, T, 4)
    P.finalize()
    P.replay(nc, sems)
    stack.close()
    return nc, c, P


def build_l0_program(n_layers=4, taps=(), scratch_kind="ExternalOutput",
                     stop_after=None, include_ffn=False):
    nc = bass.Bass("TRN2", target_bir_lowering=False, debug=False)
    c = Ctx()
    c.scratch_kind = scratch_kind
    build_ctx(nc, c, taps=taps)
    P = Prog()
    stack = contextlib.ExitStack()
    sems = {s: stack.enter_context(nc.semaphore(s)) for s in ALLSEMS}
    phase_load_consts(P, c)
    if stop_after == "consts":
        P.finalize(); P.replay(nc, sems); stack.close(); return nc, P
    phase_load_ln0(P, c, sub=(int(stop_after[3:]) if stop_after and stop_after.startswith("ln0") and len(stop_after) > 3 else 4))
    if stop_after and stop_after.startswith("ln0"):
        P.finalize(); P.replay(nc, sems); stack.close(); return nc, P
    if "xn" in c.taps:
        emit_tap_z(P, c, "xn")
    phase_l0_weights(P, c, 0)
    for i in range(n_layers):
        phase_l0_layer(P, c, i)
        if i + 1 < n_layers:
            phase_l0_weights(P, c, i + 1)
        tp = "z%d" % (i + 1)
        if tp in c.taps:
            emit_tap_z(P, c, tp)
    phase_spill_z4(P, c)
    if include_ffn:
        phase_l0_ffn(P, c)
    P.finalize()
    P.replay(nc, sems)
    stack.close()
    return nc, P


def host_cols(inp):
    colmap, colspan, ncols = build_colmap()
    out = np.zeros((128, ncols), dtype=np.float32)

    def put(name, i, vec):
        base = colmap[name] + i * colspan[name]
        v = np.ascontiguousarray(np.asarray(vec, dtype=np.float32)).reshape(-1, 128).T
        out[:, base:base + v.shape[1]] = v

    for i in range(4):
        put("l0_ln1_g", i, inp["L0_attn_ln1_g"][i])
        put("l0_ln1_b", i, inp["L0_attn_ln1_b"][i])
        put("l0_ln2_g", i, inp["L0_attn_ln2_g"][i])
        put("l0_ln2_b", i, inp["L0_attn_ln2_b"][i])
        put("l0_bqk", i, np.concatenate([np.asarray(inp["L0_attn_bq"][i]),
                                         np.asarray(inp["L0_attn_bk"][i])]))
        put("l0_bp", i, inp["L0_attn_bp"][i])
        put("l0_bf", i, inp["L0_attn_bf"][i])
        put("l0_b1", i, inp["L0_attn_b1"][i])
        put("l0_b2", i, inp["L0_attn_b2"][i])
    put("l1_bq", 0, inp["L1_attn_bq"])
    put("l1_bk", 0, inp["L1_attn_bk"])
    put("l1_sh_b1", 0, inp["L1_ffn_shared_b1"])
    put("l1_sh_b2", 0, inp["L1_ffn_shared_b2"])
    for e in range(8):
        put("l1_sp_b1", e, inp["L1_ffn_spec_b1"][e])
        put("l1_sp_b2", e, inp["L1_ffn_spec_b2"][e])
    return out


def host_inputs(inp, b):
    f32 = lambda a: np.asarray(a, np.float32)
    cols = host_cols(inp)
    wg = np.ascontiguousarray(
        f32(inp["L0_attn_w_g"]).reshape(4, 2, 128).transpose(2, 0, 1).reshape(128, 8))
    return {
        "x": np.ascontiguousarray(f32(inp["L0_attn_x_in"][b])),
        "l0_Wq": f32(inp["L0_attn_Wq"]), "l0_Wk": f32(inp["L0_attn_Wk"]),
        "l0_Wp": f32(inp["L0_attn_Wp"]), "l0_Wf": f32(inp["L0_attn_Wf"]),
        "l0_W1": f32(inp["L0_attn_W1"]), "l0_W2": f32(inp["L0_attn_W2"]),
        "f0_rwT": np.ascontiguousarray(f32(inp["L0_ffn_router_w"]).T),
        "f0_w1": f32(inp["L0_ffn_w1"]), "f0_w2": f32(inp["L0_ffn_w2"]),
        "l1_Wq": f32(inp["L1_attn_Wq"]), "l1_Wk": f32(inp["L1_attn_Wk"]),
        "f1_rwT": np.ascontiguousarray(f32(inp["L1_ffn_router_w"]).T),
        "f1_w1T": np.ascontiguousarray(np.concatenate(
            [f32(inp["L1_ffn_shared_w1"]).T[None],
             f32(inp["L1_ffn_spec_w1"]).transpose(0, 2, 1)], 0)),
        "f1_w2T": np.ascontiguousarray(np.concatenate(
            [f32(inp["L1_ffn_shared_w2"]).T[None],
             f32(inp["L1_ffn_spec_w2"]).transpose(0, 2, 1)], 0)),
        "cols": cols,
        "l0_wg_cols": wg,
        "f1_spb2": f32(inp["L1_ffn_spec_b2"]),
        "f1_rb": f32(inp["L1_ffn_router_b"]).reshape(1, 8),
    }


_CACHE = {}


def kernel(**inputs):
    if "nc" not in _CACHE:
        _CACHE["nc"], _CACHE["c"], _CACHE["P"] = build_program()
    nc = _CACHE["nc"]
    in_maps = [host_inputs(inputs, b) for b in range(8)]
    res = run_bass_kernel_spmd(nc, in_maps, core_ids=list(range(8)))
    return np.stack([np.asarray(res.results[b]["out"]) for b in range(8)])


# version 02
# speedup vs baseline: 1.3202x; 1.3202x over previous
import contextlib
import numpy as np
import ml_dtypes
import concourse.bass as bass
import concourse.mybir as mybir
from concourse.bass_utils import run_bass_kernel_spmd

F32 = mybir.dt.float32
BF16 = mybir.dt.bfloat16
U8 = mybir.dt.uint8
AF = mybir.ActivationFunctionType
ALU = mybir.AluOpType
AXX = mybir.AxisListType.X
BF = ml_dtypes.bfloat16

D = 1024
T = 4096
NCH = 8
J8 = 512
NJ8 = T // J8
J32 = 128
NJ32 = T // J32
ADIM = 256
HID0 = 4096
EH0 = 2048
FF1 = 4096
EPS_OLN = 1e-5
EPS_ILN = 1e-6
EPS_NRM = 1e-12
SCALE_A = 0.25
INVSQD = 1.0 / 32.0

PAGE = 256
HPAGE = 1 << 16

ENGINES = ("tensor", "vector", "scalar", "gpsimd", "sync")
ESEM = {"tensor": "T", "vector": "V", "scalar": "A", "gpsimd": "G"}
DMAQ = {"gpsimd": "gd", "sync": "sd", "scalar": "ad"}
NPOOL = 6
DMAPOOL = tuple(f"{q}{i}" for q in ("gd", "sd", "ad") for i in range(NPOOL))
ALLSEMS = ("T", "V", "A", "G") + DMAPOOL
SELF_SYNC = {"vector", "scalar", "gpsimd", "sync"}


class Prog:
    def __init__(self):
        self.items = {e: [] for e in ENGINES}
        self.cnt = {s: 0 for s in ALLSEMS}
        self.clock = {e: {s: 0 for s in ALLSEMS} for e in ENGINES}
        self.snap = {}
        self.pw = {}
        self.pr = {}
        self.pending = {e: [] for e in ENGINES}
        self.nops = 0
        self.dcur = {"gd": 0, "sd": 0, "ad": 0}
        self.ddirty = {"gd": False, "sd": False, "ad": False}
        self.pend_pages = {e: set() for e in ENGINES}

    @staticmethod
    def pages(spans):
        out = []
        for sp, lo, hi in spans:
            pg = HPAGE if sp.startswith("H:") else (2048 if sp == "PS" else PAGE)
            out.extend((sp, p) for p in range(lo // pg, (hi + pg - 1) // pg))
        return out

    def _wait(self, eng, needed):
        ck = self.clock[eng]
        for sem, val in sorted(needed.items()):
            if ck.get(sem, 0) >= val:
                continue
            self.items[eng].append(("wait", sem, val))
            ck[sem] = val
            sn = self.snap.get((sem, val))
            if sn:
                for s2, v2 in sn.items():
                    if ck.get(s2, 0) < v2:
                        ck[s2] = v2

    def op(self, eng, fn, reads=(), writes=(), inc=True, dma=False):
        rp = self.pages(reads)
        wp = self.pages(writes)
        for e2 in ENGINES:
            if e2 == eng or not self.pend_pages[e2]:
                continue
            for p in wp:
                assert p not in self.pend_pages[e2], (
                    "write to page touched by pending group", eng, e2, p)
            for p in rp:
                assert p not in self.pend_pages[e2], (
                    "read of page written/read by pending group", eng, e2, p)
        needed = {}

        def add(sem, val):
            if val > needed.get(sem, 0):
                needed[sem] = val

        for p in rp:
            lw = self.pw.get(p)
            if lw:
                add(*lw)
        for p in wp:
            lw = self.pw.get(p)
            if lw:
                add(*lw)
            for sem, val in self.pr.get(p, {}).items():
                add(sem, val)
        if eng == "tensor" and not dma and "T" in needed:
            del needed["T"]
        for s in list(needed):
            if s[:2] in ("gd", "sd", "ad") and len(s) == 3:
                needed[s] = self.cnt[s]
                q = s[:2]
                if s == f"{q}{self.dcur[q]}":
                    self.ddirty[q] = True
        self._wait(eng, needed)
        if dma:
            q = DMAQ[eng]
            if self.ddirty[q]:
                idx = (self.dcur[q] + 1) % NPOOL
                s2 = f"{q}{idx}"
                if self.cnt[s2] > 0:
                    self._wait(eng, {s2: self.cnt[s2]})
                self.dcur[q] = idx
                self.ddirty[q] = False
            own = f"{q}{self.dcur[q]}"
        else:
            own = ESEM.get(eng)

        amt = 16 if dma else 1
        self.items[eng].append(("op", fn, own if inc else None, amt if inc else 0))
        self.nops += 1
        if inc:
            assert own is not None
            self.cnt[own] += amt
            v = self.cnt[own]
            ck = self.clock[eng]
            if eng not in SELF_SYNC and ck.get(own, 0) < v:
                ck[own] = v
            self.snap[(own, v)] = dict(ck, **{own: v})
            mark = (own, v)
            todo = self.pending[eng] + [(rp, wp)]
            self.pending[eng] = []
            self.pend_pages[eng] = set()
            for rps, wps in todo:
                for p in wps:
                    self.pw[p] = mark
                    self.pr[p] = {}
                for p in rps:
                    d = self.pr.setdefault(p, {})
                    if v > d.get(own, 0):
                        d[own] = v
        else:
            self.pending[eng].append((rp, wp))
            self.pend_pages[eng].update(rp)
            self.pend_pages[eng].update(wp)

    def finalize(self, eng="sync"):
        assert all(not p for p in self.pending.values())
        self._wait(eng, {s: self.cnt[s] for s in ALLSEMS if self.cnt[s] > 0})

    def replay(self, nc, sems):
        with nc.Block() as block:
            def mk(eng_name):
                def body(eng):
                    for it in self.items[eng_name]:
                        if it[0] == "wait":
                            eng.wait_ge(sems[it[1]], it[2])
                        else:
                            _, fn, sem, amt = it
                            ins = fn(eng)
                            if sem is not None:
                                ins.then_inc(sems[sem], amt)
                return body
            block.tensor(mk("tensor"))
            block.vector(mk("vector"))
            block.scalar(mk("scalar"))
            block.gpsimd(mk("gpsimd"))
            block.sync(mk("sync"))


class Ctx:
    def __init__(self):
        self.sboff = {}
        self.psoff = {}

    def _span(self, table, space, h, idx):
        name = h.name
        off, size = table[name]
        if not idx:
            return (space, off, off + size)
        shape = list(h.shape[1:])
        esz = mybir.dt.size(h.dtype)
        strides = []
        acc = esz
        for d in reversed(shape):
            strides.insert(0, acc)
            acc *= d
        base = 0
        cur = acc
        for k, it in enumerate(idx):
            if it is None:
                break
            st = strides[k]
            if isinstance(it, int):
                base += it * st
                cur = st
            else:
                a, b = it
                base += a * st
                cur = (b - a) * st
        return (space, off + base, off + base + cur)

    def reg(self, h, *idx):
        return self._span(self.sboff, "SB", h, idx)

    def regp(self, h, *idx):
        return self._span(self.psoff, "PS", h, idx)


def hreg2(name, ncol, esz, r0, r1, c0, c1):
    lo = (r0 * ncol + c0) * esz
    hi = ((r1 - 1) * ncol + c1) * esz
    return ("H:" + name, lo, hi)


def hregall(name):
    return ("H:" + name, 0, 1 << 40)


def mm(P, out_ap, lhsT_ap, rhs_ap, reads, wr, start, stop, solo=False):
    P.op("tensor",
         lambda t, o=out_ap, l=lhsT_ap, r=rhs_ap, a=start, b=stop:
             t.matmul(o, l, r, start=a, stop=b),
         reads=reads, writes=(wr,), inc=(stop or solo))


def transp(P, out_ap, in_ap, ident_ap, reads, wr):
    P.op("tensor",
         lambda t, o=out_ap, i=in_ap, d=ident_ap: t.transpose(o, i, d),
         reads=reads, writes=(wr,), inc=True)


def act(P, out_ap, in_ap, func, reads, writes, bias=0.0, scale=1.0, accum=None):
    P.op("scalar",
         lambda s, o=out_ap, i=in_ap, f=func, b=bias, sc=scale, a=accum:
             (s.activation(o, i, f, bias=b, scale=sc) if a is None else
              s.activation(o, i, f, bias=b, scale=sc, accum_out=a)),
         reads=reads, writes=writes)


def vtt(P, out_ap, a_ap, b_ap, opk, reads, writes, eng="vector"):
    P.op(eng,
         lambda v, o=out_ap, a=a_ap, b=b_ap, k=opk: v.tensor_tensor(o, a, b, k),
         reads=reads, writes=writes)


def vts(P, out_ap, in_ap, s1, s2, op0, op1, reads, writes, eng="vector"):
    if s2 is None:
        P.op(eng,
             lambda v, o=out_ap, i=in_ap, a=s1, k0=op0:
                 v.tensor_scalar(o, i, a, None, k0),
             reads=reads, writes=writes)
    else:
        P.op(eng,
             lambda v, o=out_ap, i=in_ap, a=s1, b=s2, k0=op0, k1=op1:
                 v.tensor_scalar(o, i, a, b, k0, k1),
             reads=reads, writes=writes)


def vsts(P, out_ap, in_ap, s_ap, opk, reads, writes, eng="vector"):
    P.op(eng,
         lambda v, o=out_ap, i=in_ap, s=s_ap, k=opk:
             v.tensor_single_scalar(o, i, s, k),
         reads=reads, writes=writes)


def vstt(P, out_ap, in0_ap, scal, in1_ap, op0, op1, reads, writes, eng="vector"):
    P.op(eng,
         lambda v, o=out_ap, i0=in0_ap, s=scal, i1=in1_ap, k0=op0, k1=op1:
             v.scalar_tensor_tensor(o, i0, s, i1, k0, k1),
         reads=reads, writes=writes)


def vred(P, out_ap, in_ap, opk, reads, writes, negate=False):
    P.op("vector",
         lambda v, o=out_ap, i=in_ap, k=opk, n=negate:
             v.tensor_reduce(o, i, AXX, k, negate=n),
         reads=reads, writes=writes)


def vcopy(P, out_ap, in_ap, reads, writes, eng="vector"):
    P.op(eng, lambda v, o=out_ap, i=in_ap: v.tensor_copy(o, i),
         reads=reads, writes=writes)


def vrecip(P, out_ap, in_ap, reads, writes):
    P.op("vector", lambda v, o=out_ap, i=in_ap: v.reciprocal(o, i),
         reads=reads, writes=writes)


def dma(P, eng, out_ap, in_ap, reads, writes, accum=False, slow=False):
    kw = {}
    if accum:
        kw["accum_op"] = ALU.add
    if slow:
        kw["allow_slow_non_contiguous"] = True
    P.op(eng,
         lambda g, o=out_ap, i=in_ap, k=dict(kw): g.dma_start(out=o, in_=i, **k),
         reads=reads, writes=writes, dma=True)


COLSPEC = [
    ("l0_ln1_g", 8, 4), ("l0_ln1_b", 8, 4),
    ("l0_ln2_g", 8, 4), ("l0_ln2_b", 8, 4),
    ("l0_bqk", 4, 4),
    ("l0_bp", 2, 4), ("l0_bf", 8, 4),
    ("l0_b1", 32, 4), ("l0_b2", 8, 4),
    ("l1_bq", 8, 1), ("l1_bk", 8, 1),
    ("l1_sh_b1", 32, 1), ("l1_sh_b2", 8, 1),
    ("l1_sp_b1", 32, 8), ("l1_sp_b2", 8, 8),
]


def build_colmap():
    colmap, colspan = {}, {}
    off = 0
    for name, span, cnt in COLSPEC:
        colmap[name] = off
        colspan[name] = span
        off += span * cnt
    return colmap, colspan, off


def build_ctx(nc, c, taps=()):
    colmap, colspan, ncols = build_colmap()
    c.colmap, c.colspan, c.NCOLS = colmap, colspan, ncols

    def din(name, shape, dtype=F32):
        return nc.dram_tensor(name, list(shape), dtype, kind="ExternalInput").ap()

    c.x_in = din("x", [T, D])
    c.w_l0_Wq = din("l0_Wq", [4, D, ADIM])
    c.w_l0_Wk = din("l0_Wk", [4, D, ADIM])
    c.w_l0_Wp = din("l0_Wp", [4, ADIM, ADIM])
    c.w_l0_Wf = din("l0_Wf", [4, ADIM, D])
    c.w_l0_W1 = din("l0_W1", [4, D, HID0])
    c.w_l0_W2 = din("l0_W2", [4, HID0, D])
    c.w_f0_rwT = din("f0_rwT", [D, 8])
    c.w_f0_w1 = din("f0_w1", [8, D, EH0])
    c.w_f0_w2 = din("f0_w2", [8, EH0, D])
    c.w_l1_Wq = din("l1_Wq", [D, D])
    c.w_l1_Wk = din("l1_Wk", [D, D])
    c.w_f1_rwT = din("f1_rwT", [D, 8])
    c.w_f1_w1T = din("f1_w1T", [9, D, FF1])
    c.w_f1_w2T = din("f1_w2T", [9, FF1, D])
    c.cols_in = din("cols", [128, ncols])
    c.wg_in = din("l0_wg_cols", [128, 8])
    c.spb2_in = din("f1_spb2", [8, D])
    c.rb_in = din("f1_rb", [1, 8])

    c.out = nc.dram_tensor("out", [T, D], F32, kind="ExternalOutput").ap()
    kind = c.scratch_kind
    c.acc_fm = nc.dram_tensor("acc_fm", [D, T], F32, kind=kind).ap()
    c.acc_tm = nc.dram_tensor("acc_tm", [T, D], F32, kind=kind).ap()
    scr = getattr(c, "scr_kind", "Internal")
    c.q_scr = nc.dram_tensor("q_scr", [D, T], BF16, kind="Internal").ap()
    c.arow_scr = nc.dram_tensor("arow_scr", [1, T], BF16, kind="Internal").ap()
    c.xnfm_scr = nc.dram_tensor("xnfm_scr", [D, T], BF16, kind=scr).ap()

    c.taps = {}
    for tap in taps:
        c.taps[tap] = nc.dram_tensor("tap_" + tap, [D, T], F32,
                                     kind="ExternalOutput").ap()

    c.ident_d = nc.inline_tensor(np.eye(128, dtype=BF), "identc")
    c.ident32_d = nc.inline_tensor(np.eye(128, dtype=np.float32), "identc32")
    c.ones_d = nc.inline_tensor(np.ones((128, 1), dtype=BF), "onescol_c")
    c.onesr_d = nc.inline_tensor(np.ones((1, 128), dtype=BF), "onesrow_c")

    def salloc(name, shape, dtype):
        h = nc.alloc_sbuf_tensor(name, list(shape), dtype)
        mloc = nc.lookup_mloc(h)
        c.sboff[h.name] = (mloc.addr, int(np.prod(shape[1:])) * mybir.dt.size(dtype))
        return h

    c.ident = salloc("ident", [128, 128], BF16)
    c.ident32 = salloc("ident32", [128, 128], F32)
    c.onescol = salloc("onescol", [128, 1], BF16)
    c.onesrow = salloc("onesrow", [1, 128], BF16)
    c.constf = salloc("constf", [128, ncols], F32)
    c.wgcol = salloc("wgcol", [128, 4, 2], BF16)
    c.spb2 = salloc("spb2", [8, D], BF16)
    c.rbb = salloc("rbb", [128, 8], F32)

    usable = nc.sbuf_bytes_remaining - 64
    c.arena = nc.alloc_sbuf_tensor("arena", [128, usable], U8)
    c.abase = nc.lookup_mloc(c.arena).addr

    def place(name, shape, dtype, off):
        per = int(np.prod(shape[1:])) * mybir.dt.size(dtype)
        assert off + per <= usable, (name, off, per, usable)
        h = nc.virtual_sbuf_tensor(name, list(shape), dtype)
        nc._tensor(h.name, h.shape, h.dtype, type="SB")
        mloc = nc.lookup_mloc(h)
        mloc.allocated = True
        mloc.addr = c.abase + off
        c.sboff[h.name] = (c.abase + off, per)
        return h

    c.place = place
    K = 1024
    c.zf = place("zf", [128, NCH, T], F32, 0)
    c.qkb = place("qkb", [128, 4, T], BF16, 128 * K)
    c.h2b = place("h2b", [128, NCH, T // 2], BF16, 128 * K)
    c.sc = place("sc", [128, 1536], F32, 160 * K)
    c.sq = place("sq", [128, 512], BF16, 166 * K)
    c.qk_sc = place("qk_sc", [128, 2, 2, J8], BF16, 167 * K)
    c.zbt = place("zbt", [128, 2, J8], BF16, 167 * K)
    c.ht = place("ht", [128, 1, NCH, J8], BF16, 171 * K)
    c.kg = place("kg", [128, 2, J8], BF16, 171 * K)
    c.outt = place("outt", [128, 2, J8], BF16, 173 * K)
    c.f1 = place("f1", [128, 4, J8], BF16, 175 * K)
    c.wqk = place("wqk", [128, NCH, 2 * ADIM], BF16, 179 * K)
    c.w1g = place("w1g", [128, NCH, 512], BF16, 179 * K)
    c.ltsq = place("ltsq", [128, 1, D], F32, 179 * K)
    c.wpf = place("wpf", [128, 2, ADIM], BF16, 187 * K)
    c.wpf2 = place("wpf2", [128, 2, D], BF16, 188 * K)
    c.w2g = place("w2g", [128, 4, D], BF16, 187 * K)
    c.lt = place("lt", [128, 1, D], F32, 195 * K)
    c.segrow = place("segrow", [1, 2, J8], BF16, 195 * K)
    c.rinvrow = place("rinvrow", [1, J8], BF16, 197 * K)
    c.aseg = place("aseg", [1, J8], BF16, 198 * K)
    c.ltstat = place("ltstat", [128, 2, 4], F32, 183 * K)
    c.gpart = place("gpart", [128, 2, NJ8], F32, 199 * K)
    c.gcol = place("gcol", [128, 2], F32, 199 * K + 128)
    c.st_one = place("st_one", [1, 16], F32, 199 * K + 256)
    c.l0_end = 200 * K
    c.xnh = place("xnh", [128, NCH, T // 2], BF16, 0)
    c.w1e = place("w1e", [128, 2, NCH, EH0], BF16, 32 * K)
    c.w2e = place("w2e", [128, 16, D], BF16, 96 * K)
    c.f1e = place("f1e", [128, 2, 16, J8], BF16, 128 * K)
    c.stg = place("stg", [128, 2, 4, J8], F32, 160 * K)
    c.swem = place("swem", [8, T // 2], BF16, 176 * K)
    c.swrow = place("swrow", [1, T // 2], BF16, 180 * K)
    c.rt = place("rt", [128, 32], F32, 184 * K)
    c.ffnsc = place("ffnsc", [128, 2, J8], F32, 185 * K)
    c.ltb2 = place("ltb2", [128, D], BF16, 189 * K)
    c.ltsq2 = place("ltsq2", [128, D], F32, 191 * K)
    c.ltstat2 = place("ltstat2", [128, 8], F32, int(199.5 * K))
    c.rwT = place("rwT", [128, NCH, 8], BF16, int(199.75 * K))
    c.xn_tm = place("xn_tm", [128, NJ32, D], BF16, 0)
    c.k_em = place("k_em", [128, NCH, T], BF16, 64 * K)
    c.wqk1 = place("wqk1", [128, 2, NCH, D], BF16, 128 * K)
    c.xnfm_t = place("xnfm_t", [128, 2, NCH, J8], BF16, 160 * K)
    c.lt1 = place("lt1", [128, D], F32, 176 * K)
    c.af1 = place("af1", [128, NCH, J32], F32, 180 * K)
    c.ltb1 = place("ltb1", [128, D], BF16, 184 * K)
    c.ltsq1 = place("ltsq1", [128, D], F32, 186 * K)
    c.lstat1 = place("lstat1", [128, 8], F32, 190 * K)
    c.stq = place("stq", [128, 2, D], BF16, int(190.5 * K))
    c.csum_r = place("csum_r", [1, D], F32, int(194.5 * K))
    c.csum_b = place("csum_b", [1, D], BF16, int(198.5 * K))
    c.S = place("S", [128, T], F32, 128 * K)
    c.Eb = place("Eb", [128, T], BF16, 144 * K)
    c.qsl = place("qsl", [128, 2, NCH, J32], BF16, 152 * K)
    c.ebt = place("ebt", [128, 2, J32], BF16, 156 * K)
    c.csb = place("csb", [128, D], F32, 160 * K)
    c.sta = place("sta", [128, 2, D], F32, 164 * K)
    c.acol = place("acol", [128, 16], F32, 172 * K)
    c.w1T = place("w1T", [128, NCH, FF1], BF16, 0)
    c.w2T = place("w2T", [128, 32, D], BF16, 64 * K)
    c.xnq = place("xnq", [128, NCH, D], BF16, 128 * K)
    c.F1f = place("F1f", [128, 32, J8], BF16, 144 * K)
    c.stg1 = place("stg1", [128, 2, 2, J8], F32, 176 * K)
    c.wem1 = place("wem1", [8, D], BF16, 184 * K)
    c.wrow1 = place("wrow1", [1, D], BF16, 186 * K)
    c.rt1 = place("rt1", [128, 32], F32, 188 * K)
    c.fsc = place("fsc", [128, 2, J8], F32, 189 * K)
    c.rwT1 = place("rwT1", [128, NCH, 8], BF16, 193 * K)
    c.rbrow = place("rbrow", [1, 8], BF16, int(193.5 * K))

    def palloc(name, shape, dtype):
        h = nc.alloc_psum_tensor(name, list(shape), dtype)
        mloc = nc.lookup_mloc(h)
        c.psoff[h.name] = (mloc.bank * 2048,
                           int(np.prod(shape[1:])) * mybir.dt.size(dtype))
        return h

    c.ps_main = palloc("ps_main", [128, 4, 512], F32)
    c.ps_aux = palloc("ps_aux", [128, 2, 512], F32)
    c.ps_tr32 = palloc("ps_tr32", [128, 2, 512], F32)
    h = nc.virtual_psum_tensor("ps_trb", [128, 2, 1024], BF16)
    nc._tensor(h.name, h.shape, h.dtype, type="PSUM")
    mloc = nc.lookup_mloc(h)
    mloc.allocated = True
    mloc.bank = nc.lookup_mloc(c.ps_tr32).bank
    c.psoff[h.name] = c.psoff[c.ps_tr32.name]
    c.ps_trb = h

    def ccol(name, i, ci=0):
        base = c.colmap[name] + i * c.colspan[name]
        return c.constf[:, base + ci:base + ci + 1]

    c.ccol = ccol
    c.r_const = c.reg(c.constf)
    return c


def rZ(c, ci, j=None):
    if j is None:
        return c.reg(c.zf, ci)
    return c.reg(c.zf, ci, (j * J8, (j + 1) * J8))


def psb(c, i):
    if i < 4:
        return c.ps_main[:, i, :], c.regp(c.ps_main, i)
    if i < 6:
        return c.ps_aux[:, i - 4, :], c.regp(c.ps_aux, i - 4)
    return c.ps_tr32[:, i - 6, :], c.regp(c.ps_tr32, i - 6)


def emit_bcast(P, c, row_ap, row_rd, ps_ap, ps_wr):
    mm(P, ps_ap, c.onesrow[0:1, :], row_ap, (row_rd, c.reg(c.onesrow)),
       ps_wr, start=True, stop=True)


def emit_seg_stats(P, c, j, srcb, nchunks, eps, denom):
    RPM2 = c.regp(c.ps_main, 2)
    RPM3 = c.regp(c.ps_main, 3)
    RSQ = c.reg(c.sq)
    r0 = c.sc[0:1, 0:512]
    r1 = c.sc[0:1, 512:1024]
    r2 = c.sc[0:1, 1024:1536]
    RR = c.reg(c.sc)
    for ci in range(nchunks):
        apc, rds = srcb(ci)
        mm(P, c.ps_main[0:1, 2, :], c.onescol[:, 0:1], apc,
           tuple(rds) + (c.reg(c.onescol),), RPM2,
           start=(ci == 0), stop=(ci == nchunks - 1), solo=True)
        act(P, c.sq[:, :], apc, AF.Square, rds, (RSQ,))
        mm(P, c.ps_main[0:1, 3, :], c.onescol[:, 0:1], c.sq[:, :],
           (RSQ, c.reg(c.onescol)), RPM3,
           start=(ci == 0), stop=(ci == nchunks - 1), solo=True)
    vts(P, r0, c.ps_main[0:1, 2, :], 1.0 / denom, None, ALU.mult, ALU.mult,
        (RPM2,), (RR,))
    vts(P, r1, c.ps_main[0:1, 3, :], 1.0 / denom, None, ALU.mult, ALU.mult,
        (RPM3,), (RR,))
    vtt(P, r2, r0, r0, ALU.mult, (RR,), (RR,))
    vtt(P, r1, r1, r2, ALU.subtract, (RR,), (RR,))
    vts(P, r1, r1, float(eps), None, ALU.add, ALU.add, (RR,), (RR,))
    act(P, r1, r1, AF.Sqrt, (RR,), (RR,))
    vrecip(P, r1, r1, (RR,), (RR,))
    RSEG = c.reg(c.segrow)
    act(P, c.segrow[0:1, 0, :], r0, AF.Copy, (RR,), (RSEG,))
    act(P, c.segrow[0:1, 1, :], r1, AF.Copy, (RR,), (RSEG,))


def emit_ln_apply(P, c, j, ci, dst_ap, dst_wr, gname, bname, li):
    jss = slice(j * J8, (j + 1) * J8)
    RPA0 = c.regp(c.ps_aux, 0)
    RPA1 = c.regp(c.ps_aux, 1)
    RSEG = c.reg(c.segrow)
    if ci == 0:
        emit_bcast(P, c, c.segrow[0:1, 0, :], RSEG, c.ps_aux[:, 0, :], RPA0)
        emit_bcast(P, c, c.segrow[0:1, 1, :], RSEG, c.ps_aux[:, 1, :], RPA1)
    RSC0 = c.reg(c.sc, (0, 512))
    RSC1 = c.reg(c.sc, (512, 1024))
    vtt(P, c.sc[:, 0:512], c.zf[:, ci, jss], c.ps_aux[:, 0, :], ALU.subtract,
        (rZ(c, ci, j), RPA0), (RSC0,))
    vtt(P, c.sc[:, 512:1024], c.sc[:, 0:512], c.ps_aux[:, 1, :], ALU.mult,
        (RSC0, RPA1), (RSC1,))
    if gname is None:
        act(P, dst_ap, c.sc[:, 512:1024], AF.Copy, (RSC1,), (dst_wr,))
    else:
        vts(P, dst_ap, c.sc[:, 512:1024], c.ccol(gname, li, ci),
            c.ccol(bname, li, ci), ALU.mult, ALU.add,
            (RSC1, c.r_const), (dst_wr,))


def phase_load_consts(P, c):
    dma(P, "sync", c.ident.ap(), c.ident_d.ap(), (), (c.reg(c.ident),))
    dma(P, "sync", c.ident32.ap(), c.ident32_d.ap(), (), (c.reg(c.ident32),))
    dma(P, "sync", c.onescol.ap(), c.ones_d.ap(), (), (c.reg(c.onescol),))
    dma(P, "sync", c.onesrow.ap(), c.onesr_d.ap(), (), (c.reg(c.onesrow),))
    dma(P, "sync", c.constf.ap(), c.cols_in, (), (c.r_const,))
    dma(P, "gpsimd", c.wgcol.ap(),
        c.wg_in.rearrange("p (a b) -> p a b", b=2), (), (c.reg(c.wgcol),))
    dma(P, "gpsimd", c.spb2.ap(), c.spb2_in, (), (c.reg(c.spb2),))


def phase_load_ln0(P, c, sub=4):
    for j in range(NJ8):
        dma(P, "sync", c.acc_tm[j * J8:(j + 1) * J8, :],
            c.x_in[j * J8:(j + 1) * J8, :], (),
            (hreg2("acc_tm", D, 4, j * J8, (j + 1) * J8, 0, D),))
    if sub < 1:
        return
    for j in range(NJ32):
        pj = 0
        RT = c.reg(c.lt, pj)
        RSQ2 = c.reg(c.ltsq, pj)
        RSTT = c.reg(c.ltstat, pj)
        dma(P, "sync", c.lt[:, pj, :], c.x_in[j * J32:(j + 1) * J32, :], (), (RT,))
        if sub < 2:
            continue
        vred(P, c.ltstat[:, pj, 0:1], c.lt[:, pj, :], ALU.add, (RT,), (RSTT,))
        act(P, c.ltsq[:, pj, :], c.lt[:, pj, :], AF.Square, (RT,), (RSQ2,))
        vred(P, c.ltstat[:, pj, 1:2], c.ltsq[:, pj, :], ALU.add, (RSQ2,), (RSTT,))
        vts(P, c.ltstat[:, pj, 0:1], c.ltstat[:, pj, 0:1], 1.0 / D, None,
            ALU.mult, ALU.mult, (RSTT,), (RSTT,))
        vts(P, c.ltstat[:, pj, 1:2], c.ltstat[:, pj, 1:2], 1.0 / D, None,
            ALU.mult, ALU.mult, (RSTT,), (RSTT,))
        vtt(P, c.ltstat[:, pj, 2:3], c.ltstat[:, pj, 0:1], c.ltstat[:, pj, 0:1],
            ALU.mult, (RSTT,), (RSTT,))
        vtt(P, c.ltstat[:, pj, 2:3], c.ltstat[:, pj, 1:2], c.ltstat[:, pj, 2:3],
            ALU.subtract, (RSTT,), (RSTT,))
        vts(P, c.ltstat[:, pj, 2:3], c.ltstat[:, pj, 2:3], EPS_OLN, None,
            ALU.add, ALU.add, (RSTT,), (RSTT,))
        act(P, c.ltstat[:, pj, 2:3], c.ltstat[:, pj, 2:3], AF.Sqrt,
            (RSTT,), (RSTT,))
        vrecip(P, c.ltstat[:, pj, 3:4], c.ltstat[:, pj, 2:3], (RSTT,), (RSTT,))
        vts(P, c.lt[:, pj, :], c.lt[:, pj, :], c.ltstat[:, pj, 0:1],
            c.ltstat[:, pj, 3:4], ALU.subtract, ALU.mult, (RT, RSTT), (RT,))
        if sub < 3:
            continue
        for ci in range(NCH):
            pb = (j * NCH + ci) % 2
            RPT = c.regp(c.ps_tr32, pb)
            transp(P, c.ps_tr32[:, pb, 0:128], c.lt[:, pj, ci * 128:(ci + 1) * 128],
                   c.ident32[:, :], (RT, c.reg(c.ident32)), RPT)
            vcopy(P, c.zf[:, ci, j * J32:(j + 1) * J32], c.ps_tr32[:, pb, 0:128],
                  (RPT,), (c.reg(c.zf, ci, (j * J32, (j + 1) * J32)),))


def phase_l0_weights(P, c, i):
    RW = c.reg(c.wqk)
    dma(P, "gpsimd", c.wqk[:, :, 0:ADIM],
        c.w_l0_Wq[i].rearrange("(a p) n -> p a n", p=128), (), (RW,))
    dma(P, "gpsimd", c.wqk[:, :, ADIM:2 * ADIM],
        c.w_l0_Wk[i].rearrange("(a p) n -> p a n", p=128), (), (RW,))
    dma(P, "gpsimd", c.wpf[:, :, :],
        c.w_l0_Wp[i].rearrange("(a p) n -> p a n", p=128), (), (c.reg(c.wpf),))
    dma(P, "gpsimd", c.wpf2[:, :, :],
        c.w_l0_Wf[i].rearrange("(a p) n -> p a n", p=128), (), (c.reg(c.wpf2),))


def phase_l0_layer(P, c, i):
    RSQ = c.reg(c.sq)
    RSC0 = c.reg(c.sc, (0, 512))
    R_ROW = c.reg(c.sc, (1024, 1536))
    lr = c.sc[0:1, 1024:1536]
    for j in range(NJ8):
        js = (j * J8, (j + 1) * J8)
        jss = slice(*js)
        pj = j % 2

        def srcb(ci, _j=j):
            pz = ci % 2
            apc = c.zbt[:, pz, :]
            key = c.reg(c.zbt, pz)
            act(P, apc, c.zf[:, ci, _j * J8:(_j + 1) * J8], AF.Copy,
                (rZ(c, ci, _j),), (key,))
            return apc, [key]

        emit_seg_stats(P, c, j, srcb, NCH, EPS_ILN, D)
        for ci in range(NCH):
            emit_ln_apply(P, c, j, ci, c.ht[:, 0, ci, :], c.reg(c.ht, 0, ci),
                          "l0_ln1_g", "l0_ln1_b", i)
        for s in range(2):
            for oc in range(2):
                bank = oc
                RPM = c.regp(c.ps_main, bank)
                for ci in range(NCH):
                    mm(P, c.ps_main[:, bank, :],
                       c.wqk[:, ci, s * ADIM + oc * 128:s * ADIM + (oc + 1) * 128],
                       c.ht[:, 0, ci, :],
                       (c.reg(c.wqk), c.reg(c.ht, 0, ci)), RPM,
                       start=(ci == 0), stop=(ci == NCH - 1))
                vts(P, c.qk_sc[:, s, oc, :], c.ps_main[:, bank, :],
                    c.ccol("l0_bqk", i, s * 2 + oc), None, ALU.add, ALU.add,
                    (RPM, c.r_const), (c.reg(c.qk_sc, s, oc),))
            RPM2 = c.regp(c.ps_main, 2)
            for oc in range(2):
                act(P, c.sq[:, :], c.qk_sc[:, s, oc, :], AF.Square,
                    (c.reg(c.qk_sc, s, oc),), (RSQ,))
                mm(P, c.ps_main[0:1, 2, :], c.onescol[:, 0:1], c.sq[:, :],
                   (RSQ, c.reg(c.onescol)), RPM2, start=(oc == 0), stop=(oc == 1),
                   solo=True)
            act(P, lr, c.ps_main[0:1, 2, :], AF.Sqrt, (RPM2,), (R_ROW,))
            vts(P, lr, lr, EPS_NRM, None, ALU.max, ALU.max, (R_ROW,), (R_ROW,))
            vrecip(P, lr, lr, (R_ROW,), (R_ROW,))
            act(P, c.rinvrow[0:1, :], lr, AF.Copy, (R_ROW,), (c.reg(c.rinvrow),))
            emit_bcast(P, c, c.rinvrow[0:1, :], c.reg(c.rinvrow),
                       c.ps_main[:, 2, :], RPM2)
            for oc in range(2):
                ch = s * 2 + oc
                vtt(P, c.qkb[:, ch, jss], c.qk_sc[:, s, oc, :],
                    c.ps_main[:, 2, :], ALU.mult,
                    (c.reg(c.qk_sc, s, oc), RPM2), (c.reg(c.qkb, ch, js),))
        RPM3 = c.regp(c.ps_main, 3)
        for oc in range(2):
            mm(P, c.ps_main[0:1, 3, :], c.wgcol[:, i, oc:oc + 1],
               c.qkb[:, oc, jss],
               (c.reg(c.wgcol), c.reg(c.qkb, oc, js)), RPM3,
               start=(oc == 0), stop=(oc == 1))
        R1 = c.reg(c.st_one)
        vts(P, lr, c.ps_main[0:1, 3, :], SCALE_A, None, ALU.mult, ALU.mult,
            (RPM3,), (R_ROW,))
        act(P, c.aseg[0:1, :], lr, AF.Copy, (R_ROW,), (c.reg(c.aseg),))
        act(P, lr, lr, AF.Square, (R_ROW,), (R_ROW,))
        vred(P, c.st_one[0:1, 2 + j:3 + j], lr, ALU.add, (R_ROW,), (R1,))
        dma(P, "sync", c.arow_scr[0:1, jss], c.aseg[0:1, :],
            (c.reg(c.aseg),), (("H:arow_scr", j * J8 * 2, (j + 1) * J8 * 2),))

    R1 = c.reg(c.st_one)
    vred(P, c.st_one[0:1, 0:1], c.st_one[0:1, 2:10], ALU.add, (R1,), (R1,))
    act(P, c.st_one[0:1, 0:1], c.st_one[0:1, 0:1], AF.Sqrt, (R1,), (R1,))
    vts(P, c.st_one[0:1, 0:1], c.st_one[0:1, 0:1], EPS_NRM, None, ALU.max,
        ALU.max, (R1,), (R1,))
    vrecip(P, c.st_one[0:1, 1:2], c.st_one[0:1, 0:1], (R1,), (R1,))
    for j in range(NJ8):
        js = (j * J8, (j + 1) * J8)
        jss = slice(*js)
        dma(P, "sync", c.aseg[0:1, :], c.arow_scr[0:1, jss],
            (("H:arow_scr", j * J8 * 2, (j + 1) * J8 * 2),), (c.reg(c.aseg),))
        vsts(P, c.aseg[0:1, :], c.aseg[0:1, :], c.st_one[0:1, 1:2], ALU.mult,
             (c.reg(c.aseg), R1), (c.reg(c.aseg),))
        RPM2 = c.regp(c.ps_main, 2)
        emit_bcast(P, c, c.aseg[0:1, :], c.reg(c.aseg), c.ps_main[:, 2, :], RPM2)
        for oc in range(2):
            vtt(P, c.sq[:, :], c.qkb[:, oc, jss], c.ps_main[:, 2, :], ALU.mult,
                (c.reg(c.qkb, oc, js), RPM2), (RSQ,))
            vred(P, c.gpart[:, oc, j:j + 1], c.sq[:, :], ALU.add, (RSQ,),
                 (c.reg(c.gpart, oc),))
    for oc in range(2):
        vred(P, c.gcol[:, oc:oc + 1], c.gpart[:, oc, :], ALU.add,
             (c.reg(c.gpart, oc),), (c.reg(c.gcol),))

    for j in range(NJ8):
        js = (j * J8, (j + 1) * J8)
        jss = slice(*js)
        for oc in range(2):
            vts(P, c.kg[:, oc, :], c.qkb[:, 2 + oc, jss], c.gcol[:, oc:oc + 1],
                None, ALU.mult, ALU.mult,
                (c.reg(c.qkb, 2 + oc, js), c.reg(c.gcol)), (c.reg(c.kg, oc),))
        for oc in range(2):
            bank = oc
            RPM = c.regp(c.ps_main, bank)
            for kc in range(2):
                mm(P, c.ps_main[:, bank, :],
                   c.wpf[:, kc, oc * 128:(oc + 1) * 128], c.kg[:, kc, :],
                   (c.reg(c.wpf), c.reg(c.kg, kc)), RPM,
                   start=(kc == 0), stop=(kc == 1))
            vts(P, c.sc[:, 0:512], c.ps_main[:, bank, :],
                c.ccol("l0_bp", i, oc), None, ALU.add, ALU.add,
                (RPM, c.r_const), (RSC0,))
            vtt(P, c.outt[:, oc, :], c.sc[:, 0:512], c.qkb[:, oc, jss], ALU.add,
                (RSC0, c.reg(c.qkb, oc, js)), (c.reg(c.outt, oc),))
        for ocd in range(NCH):
            bank = 2 + (ocd % 2)
            RPM = c.regp(c.ps_main, bank)
            for kc in range(2):
                mm(P, c.ps_main[:, bank, :],
                   c.wpf2[:, kc, ocd * 128:(ocd + 1) * 128], c.outt[:, kc, :],
                   (c.reg(c.wpf2), c.reg(c.outt, kc)), RPM,
                   start=(kc == 0), stop=(kc == 1))
            vstt(P, c.zf[:, ocd, jss], c.ps_main[:, bank, :],
                 c.ccol("l0_bf", i, ocd), c.zf[:, ocd, jss], ALU.add, ALU.add,
                 (RPM, rZ(c, ocd, j), c.r_const), (rZ(c, ocd, j),))

    nog = HID0 // 512
    for half in range(2):
        jlo = half * (NJ8 // 2)
        for j in range(jlo, jlo + NJ8 // 2):
            jr = j - jlo
            pj = j % 2

            def srcb(ci, _j=j):
                pz = ci % 2
                apc = c.zbt[:, pz, :]
                key = c.reg(c.zbt, pz)
                act(P, apc, c.zf[:, ci, _j * J8:(_j + 1) * J8], AF.Copy,
                    (rZ(c, ci, _j),), (key,))
                return apc, [key]

            emit_seg_stats(P, c, j, srcb, NCH, EPS_ILN, D)
            for ci in range(NCH):
                emit_ln_apply(P, c, j, ci,
                              c.h2b[:, ci, jr * J8:(jr + 1) * J8],
                              c.reg(c.h2b, ci, (jr * J8, (jr + 1) * J8)),
                              "l0_ln2_g", "l0_ln2_b", i)
        for og in range(nog):
            RW1 = c.reg(c.w1g)
            RW2 = c.reg(c.w2g)
            dma(P, "gpsimd", c.w1g[:, :, :],
                c.w_l0_W1[i].rearrange("(a p) n -> p a n", p=128)[
                    :, :, og * 512:(og + 1) * 512], (), (RW1,))
            dma(P, "gpsimd", c.w2g[:, :, :],
                c.w_l0_W2[i, og * 512:(og + 1) * 512, :].rearrange(
                    "(a p) n -> p a n", p=128), (), (RW2,))
            for j in range(jlo, jlo + NJ8 // 2):
                js = (j * J8, (j + 1) * J8)
                jss = slice(*js)
                jr = j - jlo
                for oc4 in range(4):
                    bap, RPM = psb(c, (0, 1, 4)[oc4 % 3])
                    for ci in range(NCH):
                        mm(P, bap, c.w1g[:, ci, oc4 * 128:(oc4 + 1) * 128],
                           c.h2b[:, ci, jr * J8:(jr + 1) * J8],
                           (RW1, c.reg(c.h2b, ci, (jr * J8, (jr + 1) * J8))),
                           RPM, start=(ci == 0), stop=(ci == NCH - 1))
                    act(P, c.f1[:, oc4, :], bap, AF.Gelu,
                        (RPM, c.r_const), (c.reg(c.f1, oc4),),
                        bias=c.ccol("l0_b1", i, og * 4 + oc4))
                for ocd in range(NCH):
                    bap, RPM = psb(c, (2, 3, 5)[ocd % 3])
                    for kc in range(4):
                        mm(P, bap, c.w2g[:, kc, ocd * 128:(ocd + 1) * 128],
                           c.f1[:, kc, :], (RW2, c.reg(c.f1, kc)), RPM,
                           start=(kc == 0), stop=(kc == 3))
                    if og == nog - 1:
                        vstt(P, c.zf[:, ocd, jss], bap,
                             c.ccol("l0_b2", i, ocd), c.zf[:, ocd, jss],
                             ALU.add, ALU.add,
                             (RPM, rZ(c, ocd, j), c.r_const), (rZ(c, ocd, j),))
                    else:
                        vtt(P, c.zf[:, ocd, jss], bap,
                            c.zf[:, ocd, jss], ALU.add,
                            (RPM, rZ(c, ocd, j)), (rZ(c, ocd, j),))



def phase_l0_ffn(P, c):
    RRW = c.reg(c.rwT)
    for ci in range(NCH):
        dma(P, "gpsimd", c.rwT[:, ci, :],
            c.w_f0_rwT[ci * 128:(ci + 1) * 128, :], (), (RRW,))
    for half in range(2):
        h0 = half * (T // 2)
        for jj in range(16):
            j = half * 16 + jj
            tsl = slice(j * J32, (j + 1) * J32)
            hsl = slice(jj * J32, (jj + 1) * J32)
            RT = c.reg(c.lt, 0)
            RB2 = c.reg(c.ltb2)
            RSQ2 = c.reg(c.ltsq2)
            RSTT = c.reg(c.ltstat2)
            dma(P, "sync", c.lt[:, 0, :], c.x_in[tsl, :], (), (RT,))
            vred(P, c.ltstat2[:, 0:1], c.lt[:, 0, :], ALU.add, (RT,), (RSTT,))
            act(P, c.ltsq2[:, :], c.lt[:, 0, :], AF.Square, (RT,), (RSQ2,))
            vred(P, c.ltstat2[:, 1:2], c.ltsq2[:, :], ALU.add, (RSQ2,), (RSTT,))
            vts(P, c.ltstat2[:, 0:1], c.ltstat2[:, 0:1], 1.0 / D, None,
                ALU.mult, ALU.mult, (RSTT,), (RSTT,))
            vts(P, c.ltstat2[:, 1:2], c.ltstat2[:, 1:2], 1.0 / D, None,
                ALU.mult, ALU.mult, (RSTT,), (RSTT,))
            vtt(P, c.ltstat2[:, 2:3], c.ltstat2[:, 0:1], c.ltstat2[:, 0:1],
                ALU.mult, (RSTT,), (RSTT,))
            vtt(P, c.ltstat2[:, 2:3], c.ltstat2[:, 1:2], c.ltstat2[:, 2:3],
                ALU.subtract, (RSTT,), (RSTT,))
            vts(P, c.ltstat2[:, 2:3], c.ltstat2[:, 2:3], EPS_OLN, None,
                ALU.add, ALU.add, (RSTT,), (RSTT,))
            act(P, c.ltstat2[:, 2:3], c.ltstat2[:, 2:3], AF.Sqrt, (RSTT,), (RSTT,))
            vrecip(P, c.ltstat2[:, 3:4], c.ltstat2[:, 2:3], (RSTT,), (RSTT,))
            vts(P, c.lt[:, 0, :], c.lt[:, 0, :], c.ltstat2[:, 0:1],
                c.ltstat2[:, 3:4], ALU.subtract, ALU.mult, (RT, RSTT), (RT,))
            act(P, c.ltb2[:, :], c.lt[:, 0, :], AF.Copy, (RT,), (RB2,))
            for ci in range(NCH):
                RPT = c.regp(c.ps_trb, 0)
                transp(P, c.ps_trb[:, 0, 0:128], c.ltb2[:, ci * 128:(ci + 1) * 128],
                       c.ident[:, :], (RB2, c.reg(c.ident)), RPT)
                vcopy(P, c.xnh[:, ci, hsl], c.ps_trb[:, 0, 0:128], (RPT,),
                      (c.reg(c.xnh, ci, (jj * J32, (jj + 1) * J32)),))
            RPA0 = c.regp(c.ps_aux, 0)
            for ci in range(NCH):
                mm(P, c.ps_aux[:, 0, 0:8], c.xnh[:, ci, hsl], c.rwT[:, ci, :],
                   (c.reg(c.xnh, ci, (jj * J32, (jj + 1) * J32)), RRW), RPA0,
                   start=(ci == 0), stop=(ci == NCH - 1), solo=True)
            RRT = c.reg(c.rt)
            lg = c.rt[:, 0:8]
            vcopy(P, lg, c.ps_aux[:, 0, 0:8], (RPA0,), (RRT,))
            m1 = c.rt[:, 8:9]
            m2 = c.rt[:, 9:10]
            msk = c.rt[:, 10:18]
            rden = c.rt[:, 11:12]
            rden = c.rt[:, 18:19]
            negm1 = c.rt[:, 19:20]
            eterm = c.rt[:, 20:28]
            vred(P, m1, lg, ALU.max, (RRT,), (RRT,))
            vts(P, msk, lg, m1, None, ALU.is_equal, ALU.is_equal, (RRT,), (RRT,))
            vstt(P, msk, msk, -1e30, lg, ALU.mult, ALU.add, (RRT,), (RRT,))
            vred(P, m2, msk, ALU.max, (RRT,), (RRT,))
            vtt(P, c.rt[:, 28:29], m2, m1, ALU.subtract, (RRT,), (RRT,))
            act(P, c.rt[:, 28:29], c.rt[:, 28:29], AF.Exp, (RRT,), (RRT,))
            vts(P, c.rt[:, 28:29], c.rt[:, 28:29], 1.0, None, ALU.add, ALU.add,
                (RRT,), (RRT,))
            vrecip(P, rden, c.rt[:, 28:29], (RRT,), (RRT,))
            vts(P, negm1, m1, -1.0, None, ALU.mult, ALU.mult, (RRT,), (RRT,))
            act(P, eterm, lg, AF.Exp, (RRT,), (RRT,), bias=negm1)
            vts(P, c.rt[:, 10:18], lg, m2, None, ALU.is_ge, ALU.is_ge,
                (RRT,), (RRT,))
            vstt(P, eterm, eterm, rden, c.rt[:, 10:18], ALU.mult, ALU.mult,
                 (RRT,), (RRT,))
            act(P, eterm, eterm, AF.Sqrt, (RRT,), (RRT,))
            RPT2 = c.regp(c.ps_tr32, 1)
            transp(P, c.ps_tr32[0:8, 1, 0:128], eterm, c.ident32[:, :],
                   (RRT, c.reg(c.ident32)), RPT2)
            vcopy(P, c.swem[0:8, hsl], c.ps_tr32[0:8, 1, 0:128], (RPT2,),
                  (c.reg(c.swem),))
        for e in range(8):
            pe_ = e % 2
            RW1 = c.reg(c.w1e, pe_)
            RW2 = c.reg(c.w2e)
            if half == 0 and e == 0:
                dma(P, "gpsimd", c.w1e[:, 0, :, :],
                    c.w_f0_w1[0].rearrange("(a p) n -> p a n", p=128), (),
                    (c.reg(c.w1e, 0),))
            if e < 7:
                dma(P, "gpsimd", c.w1e[:, (e + 1) % 2, :, :],
                    c.w_f0_w1[e + 1].rearrange("(a p) n -> p a n", p=128), (),
                    (c.reg(c.w1e, (e + 1) % 2),))
            elif half == 0:
                dma(P, "gpsimd", c.w1e[:, 0, :, :],
                    c.w_f0_w1[0].rearrange("(a p) n -> p a n", p=128), (),
                    (c.reg(c.w1e, 0),))
            RSW = c.reg(c.swrow)
            dma(P, "sync", c.swrow[0:1, :], c.swem[e:e + 1, :],
                (c.reg(c.swem),), (RSW,))
            for jj4 in range(4):
                js_h = slice(jj4 * J8, (jj4 + 1) * J8)
                jg = half * (T // 2) + jj4 * J8
                pf = jj4 % 2
                RPA1 = c.regp(c.ps_aux, 1)
                emit_bcast(P, c, c.swrow[0:1, js_h], RSW, c.ps_aux[:, 1, :], RPA1)
                for hc in range(16):
                    bap, RPM = psb(c, (0, 1, 4)[hc % 3])
                    for ci in range(NCH):
                        mm(P, bap, c.w1e[:, pe_, ci, hc * 128:(hc + 1) * 128],
                           c.xnh[:, ci, js_h],
                           (RW1, c.reg(c.xnh, ci, (jj4 * J8, (jj4 + 1) * J8))),
                           RPM, start=(ci == 0), stop=(ci == NCH - 1))
                    RFS = c.reg(c.ffnsc, hc % 2)
                    vts(P, c.ffnsc[:, hc % 2, :], bap, 0.0,
                        None, ALU.max, ALU.max, (RPM,), (RFS,))
                    vtt(P, c.ffnsc[:, hc % 2, :], c.ffnsc[:, hc % 2, :],
                        c.ps_aux[:, 1, :], ALU.mult, (RFS, RPA1), (RFS,))
                    act(P, c.f1e[:, pf, hc, :], c.ffnsc[:, hc % 2, :], AF.Square,
                        (RFS,), (c.reg(c.f1e, pf, hc),))
                if jj4 == 0:
                    dma(P, "gpsimd", c.w2e[:, :, :],
                        c.w_f0_w2[e].rearrange("(a p) n -> p a n", p=128), (),
                        (RW2,))
                for oc in range(NCH):
                    bap, RPM = psb(c, (2, 3, 6)[oc % 3])
                    for kc in range(16):
                        mm(P, bap, c.w2e[:, kc, oc * 128:(oc + 1) * 128],
                           c.f1e[:, pf, kc, :],
                           (RW2, c.reg(c.f1e, pf, kc)), RPM,
                           start=(kc == 0), stop=(kc == 15))
                    ps_ = (oc // 4) % 2
                    vcopy(P, c.stg[:, ps_, oc % 4, :], bap,
                          (RPM,), (c.reg(c.stg, ps_, oc % 4),))
                    if oc % 4 == 3:
                        o0 = (oc // 4) * 4
                        dma(P, "gpsimd",
                            c.acc_fm.rearrange("(o p) t -> p o t", p=128)[
                                :, o0:o0 + 4, jg:jg + J8],
                            c.stg[:, ps_, :, :],
                            (c.reg(c.stg, ps_),),
                            (hreg2("acc_fm", T, 4, o0 * 128, (o0 + 4) * 128,
                                   jg, jg + J8),), accum=True)

def phase_spill_z4(P, c):
    for ci in range(NCH):
        for j in range(NJ8):
            jss = slice(j * J8, (j + 1) * J8)
            dma(P, "sync", c.acc_fm[ci * 128:(ci + 1) * 128, jss],
                c.zf[:, ci, jss], (rZ(c, ci, j),),
                (hreg2("acc_fm", T, 4, ci * 128, (ci + 1) * 128,
                       j * J8, (j + 1) * J8),))



def top2_weights(P, c, lg_psum_rd, rt, RRT, sqrt_w):
    lg = rt[:, 0:8]
    m1 = rt[:, 8:9]
    m2 = rt[:, 9:10]
    msk = rt[:, 10:18]
    rden = rt[:, 18:19]
    negm1 = rt[:, 19:20]
    eterm = rt[:, 20:28]
    tmp = rt[:, 28:29]
    vred(P, m1, lg, ALU.max, (RRT,), (RRT,))
    vts(P, msk, lg, m1, None, ALU.is_equal, ALU.is_equal, (RRT,), (RRT,))
    vstt(P, msk, msk, -1e30, lg, ALU.mult, ALU.add, (RRT,), (RRT,))
    vred(P, m2, msk, ALU.max, (RRT,), (RRT,))
    vtt(P, tmp, m2, m1, ALU.subtract, (RRT,), (RRT,))
    act(P, tmp, tmp, AF.Exp, (RRT,), (RRT,))
    vts(P, tmp, tmp, 1.0, None, ALU.add, ALU.add, (RRT,), (RRT,))
    vrecip(P, rden, tmp, (RRT,), (RRT,))
    vts(P, negm1, m1, -1.0, None, ALU.mult, ALU.mult, (RRT,), (RRT,))
    act(P, eterm, lg, AF.Exp, (RRT,), (RRT,), bias=negm1)
    vts(P, msk, lg, m2, None, ALU.is_ge, ALU.is_ge, (RRT,), (RRT,))
    vstt(P, eterm, eterm, rden, msk, ALU.mult, ALU.mult, (RRT,), (RRT,))
    if sqrt_w:
        act(P, eterm, eterm, AF.Sqrt, (RRT,), (RRT,))


def phase_l1_attn(P, c):
    accfm_v = c.acc_fm.rearrange("(c p) t -> p c t", p=128)
    RWQK = c.reg(c.wqk1)
    for s, w_in in ((0, c.w_l1_Wq), (1, c.w_l1_Wk)):
        dma(P, "gpsimd", c.wqk1[:, s, :, :],
            w_in.rearrange("(a p) n -> p a n", p=128), (), (RWQK,))
    RCS0 = c.regp(c.ps_aux, 0)
    RCS1 = c.regp(c.ps_aux, 1)
    for j in range(NJ32):
        tsl = slice(j * J32, (j + 1) * J32)
        RT = c.reg(c.lt1)
        RAF = c.reg(c.af1)
        RB = c.reg(c.ltb1)
        RSQ = c.reg(c.ltsq1)
        RSTT = c.reg(c.lstat1)
        dma(P, "sync", c.lt1[:, :], c.acc_tm[tsl, :],
            (hreg2("acc_tm", D, 4, j * J32, (j + 1) * J32, 0, D),), (RT,))
        dma(P, "sync", c.af1[:, :, :], accfm_v[:, :, tsl],
            (hreg2("acc_fm", T, 4, 0, D, j * J32, (j + 1) * J32),), (RAF,))
        for ci in range(NCH):
            pb = ci % 2
            RPT = c.regp(c.ps_tr32, pb)
            transp(P, c.ps_tr32[:, pb, 0:128], c.af1[:, ci, :], c.ident32[:, :],
                   (RAF, c.reg(c.ident32)), RPT)
            vtt(P, c.lt1[:, ci * 128:(ci + 1) * 128],
                c.lt1[:, ci * 128:(ci + 1) * 128], c.ps_tr32[:, pb, 0:128],
                ALU.add, (RT, RPT), (RT,))
        vred(P, c.lstat1[:, 0:1], c.lt1[:, :], ALU.add, (RT,), (RSTT,))
        act(P, c.ltsq1[:, :], c.lt1[:, :], AF.Square, (RT,), (RSQ,))
        vred(P, c.lstat1[:, 1:2], c.ltsq1[:, :], ALU.add, (RSQ,), (RSTT,))
        vts(P, c.lstat1[:, 0:1], c.lstat1[:, 0:1], 1.0 / D, None, ALU.mult,
            ALU.mult, (RSTT,), (RSTT,))
        vts(P, c.lstat1[:, 1:2], c.lstat1[:, 1:2], 1.0 / D, None, ALU.mult,
            ALU.mult, (RSTT,), (RSTT,))
        vtt(P, c.lstat1[:, 2:3], c.lstat1[:, 0:1], c.lstat1[:, 0:1], ALU.mult,
            (RSTT,), (RSTT,))
        vtt(P, c.lstat1[:, 2:3], c.lstat1[:, 1:2], c.lstat1[:, 2:3],
            ALU.subtract, (RSTT,), (RSTT,))
        vts(P, c.lstat1[:, 2:3], c.lstat1[:, 2:3], EPS_OLN, None, ALU.add,
            ALU.add, (RSTT,), (RSTT,))
        act(P, c.lstat1[:, 2:3], c.lstat1[:, 2:3], AF.Sqrt, (RSTT,), (RSTT,))
        vrecip(P, c.lstat1[:, 3:4], c.lstat1[:, 2:3], (RSTT,), (RSTT,))
        vts(P, c.lt1[:, :], c.lt1[:, :], c.lstat1[:, 0:1], c.lstat1[:, 3:4],
            ALU.subtract, ALU.mult, (RT, RSTT), (RT,))
        act(P, c.ltb1[:, :], c.lt1[:, :], AF.Copy, (RT,), (RB,))
        RXT = c.reg(c.xn_tm, j)
        vcopy(P, c.xn_tm[:, j, :], c.ltb1[:, :], (RB,), (RXT,), eng="gpsimd")
        mm(P, c.ps_aux[0:1, 0, :], c.onescol[:, 0:1], c.ltb1[:, 0:512],
           (RB, c.reg(c.onescol)), RCS0, start=(j == 0), stop=(j == NJ32 - 1),
           solo=True)
        mm(P, c.ps_aux[0:1, 1, :], c.onescol[:, 0:1], c.ltb1[:, 512:1024],
           (RB, c.reg(c.onescol)), RCS1, start=(j == 0), stop=(j == NJ32 - 1),
           solo=True)
        jr = (j % 4) * J32
        pj8 = (j // 4) % 2
        RXF = c.reg(c.xnfm_t, pj8)
        for ci in range(NCH):
            RPT = c.regp(c.ps_trb, ci % 2)
            transp(P, c.ps_trb[:, ci % 2, 0:128],
                   c.ltb1[:, ci * 128:(ci + 1) * 128], c.ident[:, :],
                   (RB, c.reg(c.ident)), RPT)
            vcopy(P, c.xnfm_t[:, pj8, ci, jr:jr + J32],
                  c.ps_trb[:, ci % 2, 0:128], (RPT,), (RXF,))
        if j % 4 == 3:
            jj8 = j // 4
            g0 = jj8 * J8
            for ci in range(NCH):
                dma(P, "sync", c.xnfm_scr[ci * 128:(ci + 1) * 128, g0:g0 + J8],
                    c.xnfm_t[:, pj8, ci, :], (RXF,),
                    (hreg2("xnfm_scr", T, 2, ci * 128, (ci + 1) * 128,
                           g0, g0 + J8),))
            for s in range(2):
                for oc in range(NCH):
                    bank = oc % 2
                    RPM = c.regp(c.ps_main, bank)
                    for ci in range(NCH):
                        mm(P, c.ps_main[:, bank, :],
                           c.wqk1[:, s, ci, oc * 128:(oc + 1) * 128],
                           c.xnfm_t[:, pj8, ci, :], (RWQK, RXF), RPM,
                           start=(ci == 0), stop=(ci == NCH - 1))
                    bcol = c.ccol("l1_bq" if s == 0 else "l1_bk", 0, oc)
                    if s == 0:
                        ps_ = oc % 2
                        RSTQ = c.reg(c.stq, ps_)
                        vts(P, c.stq[:, ps_, 0:512], c.ps_main[:, bank, :],
                            bcol, None, ALU.add, ALU.add,
                            (RPM, c.r_const), (RSTQ,))
                        dma(P, "sync",
                            c.q_scr[oc * 128:(oc + 1) * 128, g0:g0 + J8],
                            c.stq[:, ps_, 0:512], (RSTQ,),
                            (hreg2("q_scr", T, 2, oc * 128, (oc + 1) * 128,
                                   g0, g0 + J8),))
                    else:
                        vts(P, c.k_em[:, oc, g0:g0 + J8], c.ps_main[:, bank, :],
                            bcol, None, ALU.add, ALU.add, (RPM, c.r_const),
                            (c.reg(c.k_em, oc, (g0, g0 + J8)),))
    RCR = c.reg(c.csum_r)
    RCB = c.reg(c.csum_b)
    vcopy(P, c.csum_r[0:1, 0:512], c.ps_aux[0:1, 0, :], (RCS0,), (RCR,))
    vcopy(P, c.csum_r[0:1, 512:1024], c.ps_aux[0:1, 1, :], (RCS1,), (RCR,))
    act(P, c.csum_b[0:1, :], c.csum_r[0:1, :], AF.Copy, (RCR,), (RCB,))
    for hh in range(2):
        RPA = c.regp(c.ps_aux, hh)
        emit_bcast(P, c, c.csum_b[0:1, hh * 512:(hh + 1) * 512], RCB,
                   c.ps_aux[:, hh, :], RPA)
        vcopy(P, c.csb[:, hh * 512:(hh + 1) * 512], c.ps_aux[:, hh, :],
              (RPA,), (c.reg(c.csb),))

    RS = c.reg(c.S)
    REB = c.reg(c.Eb)
    RA = c.reg(c.acol)
    for sb in range(NJ32):
        s0 = sb * J32
        pq = sb % 2
        RQ = c.reg(c.qsl, pq)
        dma(P, "sync", c.qsl[:, pq, :, :],
            c.q_scr.rearrange("(c p) t -> p c t", p=128)[:, :, s0:s0 + J32],
            (hreg2("q_scr", T, 2, 0, D, s0, s0 + J32),), (RQ,))
        for tc in range(NJ8):
            bap, RPM = psb(c, (2, 3, 4, 5)[tc % 4])
            for ci in range(NCH):
                mm(P, bap, c.qsl[:, pq, ci, :],
                   c.k_em[:, ci, tc * J8:(tc + 1) * J8],
                   (RQ, c.reg(c.k_em, ci, (tc * J8, (tc + 1) * J8))), RPM,
                   start=(ci == 0), stop=(ci == NCH - 1))
            vts(P, c.S[:, tc * J8:(tc + 1) * J8], bap,
                INVSQD, None, ALU.mult, ALU.mult, (RPM,), (RS,))
        vred(P, c.acol[:, 0:1], c.S[:, :], ALU.max, (RS,), (RA,))
        vts(P, c.acol[:, 0:1], c.acol[:, 0:1], -1.0, None, ALU.mult, ALU.mult,
            (RA,), (RA,))
        act(P, c.Eb[:, :], c.S[:, :], AF.Exp, (RS, RA), (REB, RA),
            bias=c.acol[:, 0:1], accum=c.acol[:, 1:2])
        vrecip(P, c.acol[:, 2:3], c.acol[:, 1:2], (RA,), (RA,))
        vtt(P, c.acol[:, 3:4], c.acol[:, 1:2], c.acol[:, 2:3], ALU.mult,
            (RA,), (RA,))
        vts(P, c.acol[:, 3:4], c.acol[:, 3:4], 1.0, None, ALU.add, ALU.add,
            (RA,), (RA,))
        vts(P, c.acol[:, 3:4], c.acol[:, 3:4], -1.0 / T, None, ALU.mult,
            ALU.mult, (RA,), (RA,))
        RPM0 = c.regp(c.ps_main, 0)
        RPM1 = c.regp(c.ps_main, 1)
        for tc in range(NJ32):
            pb = tc % 2
            RPT = c.regp(c.ps_trb, pb)
            RET = c.reg(c.ebt, pb)
            transp(P, c.ps_trb[:, pb, 0:128], c.Eb[:, tc * J32:(tc + 1) * J32],
                   c.ident[:, :], (REB, c.reg(c.ident)), RPT)
            vcopy(P, c.ebt[:, pb, :], c.ps_trb[:, pb, 0:128], (RPT,), (RET,))
            mm(P, c.ps_main[:, 0, :], c.ebt[:, pb, :], c.xn_tm[:, tc, 0:512],
               (RET, c.reg(c.xn_tm, tc)), RPM0,
               start=(tc == 0), stop=(tc == NJ32 - 1), solo=True)
            mm(P, c.ps_main[:, 1, :], c.ebt[:, pb, :], c.xn_tm[:, tc, 512:1024],
               (RET, c.reg(c.xn_tm, tc)), RPM1,
               start=(tc == 0), stop=(tc == NJ32 - 1), solo=True)
        ps_ = sb % 2
        RSTA = c.reg(c.sta, ps_)
        for hh in range(2):
            RPMh = c.regp(c.ps_main, hh)
            hs = slice(hh * 512, (hh + 1) * 512)
            vts(P, c.sta[:, ps_, hs], c.ps_main[:, hh, :], c.acol[:, 2:3],
                None, ALU.mult, ALU.mult, (RPMh, RA), (RSTA,))
            vstt(P, c.sta[:, ps_, hs], c.csb[:, hs], c.acol[:, 3:4],
                 c.sta[:, ps_, hs], ALU.mult, ALU.add,
                 (c.reg(c.csb), RA, RSTA), (RSTA,))
            vtt(P, c.sta[:, ps_, hs], c.sta[:, ps_, hs], c.xn_tm[:, sb, hs],
                ALU.add, (RSTA, c.reg(c.xn_tm, sb)), (RSTA,))
        dma(P, "gpsimd", c.acc_tm[s0:s0 + J32, :], c.sta[:, ps_, :],
            (RSTA,), (hreg2("acc_tm", D, 4, s0, s0 + J32, 0, D),), accum=True)


def phase_l1_ffn(P, c):
    xnv = c.xnfm_scr.rearrange("(c p) t -> p c t", p=128)
    RRW = c.reg(c.rwT1)
    for ci in range(NCH):
        dma(P, "gpsimd", c.rwT1[:, ci, :],
            c.w_f1_rwT[ci * 128:(ci + 1) * 128, :], (), (RRW,))
    dma(P, "gpsimd", c.rbrow.ap(), c.rb_in, (), (c.reg(c.rbrow),))
    RPAB = c.regp(c.ps_aux, 1)
    emit_bcast(P, c, c.rbrow[0:1, :], c.reg(c.rbrow), c.ps_aux[:, 1, 0:8], RPAB)
    vcopy(P, c.rbb[:, :], c.ps_aux[:, 1, 0:8], (RPAB,), (c.reg(c.rbb),))
    for q in range(4):
        q0 = q * D
        RXQ = c.reg(c.xnq)
        for ci in range(NCH):
            dma(P, "sync", c.xnq[:, ci, :], xnv[:, ci, q0:q0 + D],
                (hreg2("xnfm_scr", T, 2, ci * 128, (ci + 1) * 128, q0, q0 + D),),
                (RXQ,))
        RRT = c.reg(c.rt1)
        RPA0 = c.regp(c.ps_aux, 0)
        RWEM = c.reg(c.wem1)
        for t8 in range(8):
            t0 = t8 * J32
            for ci in range(NCH):
                mm(P, c.ps_aux[:, 0, 0:8], c.xnq[:, ci, t0:t0 + J32],
                   c.rwT1[:, ci, :], (RXQ, RRW), RPA0,
                   start=(ci == 0), stop=(ci == NCH - 1), solo=True)
            vtt(P, c.rt1[:, 0:8], c.ps_aux[:, 0, 0:8], c.rbb[:, :], ALU.add,
                (RPA0, c.reg(c.rbb)), (RRT,))
            top2_weights(P, c, None, c.rt1, RRT, sqrt_w=False)
            RPT2 = c.regp(c.ps_tr32, 1)
            transp(P, c.ps_tr32[0:8, 1, 0:128], c.rt1[:, 20:28], c.ident32[:, :],
                   (RRT, c.reg(c.ident32)), RPT2)
            vcopy(P, c.wem1[0:8, t0:t0 + J32], c.ps_tr32[0:8, 1, 0:128],
                  (RPT2,), (RWEM,))
        for tt2 in range(2):
            ts0 = tt2 * J8
            jg = q0 + ts0
            for oc in range(NCH):
                bank = oc % 2
                RPM = c.regp(c.ps_main, bank)
                mm(P, c.ps_main[:, bank, :], c.spb2[0:8, oc * 128:(oc + 1) * 128],
                   c.wem1[0:8, ts0:ts0 + J8], (c.reg(c.spb2), RWEM), RPM,
                   start=True, stop=True)
                ps_ = oc % 2
                RST = c.reg(c.stg1, ps_)
                vts(P, c.stg1[:, ps_, 0, :], c.ps_main[:, bank, :],
                    c.ccol("l1_sh_b2", 0, oc), None, ALU.add, ALU.add,
                    (RPM, c.r_const), (RST,))
                dma(P, "gpsimd",
                    c.acc_fm.rearrange("(o p) t -> p o t", p=128)[
                        :, oc:oc + 1, jg:jg + J8],
                    c.stg1[:, ps_, 0:1, :], (RST,),
                    (hreg2("acc_fm", T, 4, oc * 128, (oc + 1) * 128,
                           jg, jg + J8),), accum=True)
        for m in range(9):
            RW1 = c.reg(c.w1T)
            RW2 = c.reg(c.w2T)
            dma(P, "gpsimd", c.w1T[:, :, :],
                c.w_f1_w1T[m].rearrange("(a p) n -> p a n", p=128), (), (RW1,))
            for kh in range(2):
                dma(P, "gpsimd", c.w2T[:, kh * 16:(kh + 1) * 16, :],
                    c.w_f1_w2T[m, kh * 2048:(kh + 1) * 2048, :].rearrange(
                        "(a p) n -> p a n", p=128), (), (RW2,))
            if m > 0:
                RWR = c.reg(c.wrow1)
                dma(P, "sync", c.wrow1[0:1, :], c.wem1[m - 1:m, :],
                    (RWEM,), (RWR,))
            for tt2 in range(2):
                ts0 = tt2 * J8
                jg = q0 + ts0
                if m > 0:
                    RPA1 = c.regp(c.ps_aux, 1)
                    emit_bcast(P, c, c.wrow1[0:1, ts0:ts0 + J8], RWR,
                               c.ps_aux[:, 1, :], RPA1)
                for fc in range(32):
                    bap, RPM = psb(c, (0, 1, 6)[fc % 3])
                    for ci in range(NCH):
                        mm(P, bap, c.w1T[:, ci, fc * 128:(fc + 1) * 128],
                           c.xnq[:, ci, ts0:ts0 + J8], (RW1, RXQ), RPM,
                           start=(ci == 0), stop=(ci == NCH - 1))
                    bcol = c.ccol("l1_sh_b1" if m == 0 else "l1_sp_b1",
                                  0 if m == 0 else m - 1, fc)
                    RF1 = c.reg(c.F1f, fc)
                    if m == 0:
                        act(P, c.F1f[:, fc, :], bap,
                            AF.Gelu_apprx_tanh, (RPM, c.r_const), (RF1,),
                            bias=bcol)
                    else:
                        RFS = c.reg(c.fsc, fc % 2)
                        act(P, c.fsc[:, fc % 2, :], bap,
                            AF.Gelu_apprx_tanh, (RPM, c.r_const), (RFS,),
                            bias=bcol)
                        vtt(P, c.F1f[:, fc, :], c.fsc[:, fc % 2, :],
                            c.ps_aux[:, 1, :], ALU.mult, (RFS, RPA1), (RF1,))
                for oc in range(NCH):
                    bap, RPM = psb(c, (2, 3, 7)[oc % 3])
                    for kc in range(32):
                        mm(P, bap, c.w2T[:, kc, oc * 128:(oc + 1) * 128],
                           c.F1f[:, kc, :], (RW2, c.reg(c.F1f, kc)), RPM,
                           start=(kc == 0), stop=(kc == 31))
                    ps_ = (oc // 2) % 2
                    oo = oc % 2
                    RST = c.reg(c.stg1, ps_)
                    vcopy(P, c.stg1[:, ps_, oo, :], bap,
                          (RPM,), (RST,))
                    if oo == 1:
                        o0 = (oc // 2) * 2
                        dma(P, "gpsimd",
                            c.acc_fm.rearrange("(o p) t -> p o t", p=128)[
                                :, o0:o0 + 2, jg:jg + J8],
                            c.stg1[:, ps_, :, :], (RST,),
                            (hreg2("acc_fm", T, 4, o0 * 128, (o0 + 2) * 128,
                                   jg, jg + J8),), accum=True)


def phase_final(P, c):
    accfm_v = c.acc_fm.rearrange("(c p) t -> p c t", p=128)
    for j in range(NJ32):
        tsl = slice(j * J32, (j + 1) * J32)
        RT = c.reg(c.lt1)
        RAF = c.reg(c.af1)
        dma(P, "sync", c.lt1[:, :], c.acc_tm[tsl, :],
            (hreg2("acc_tm", D, 4, j * J32, (j + 1) * J32, 0, D),), (RT,))
        dma(P, "sync", c.af1[:, :, :], accfm_v[:, :, tsl],
            (hreg2("acc_fm", T, 4, 0, D, j * J32, (j + 1) * J32),), (RAF,))
        for ci in range(NCH):
            pb = ci % 2
            RPT = c.regp(c.ps_tr32, pb)
            transp(P, c.ps_tr32[:, pb, 0:128], c.af1[:, ci, :], c.ident32[:, :],
                   (RAF, c.reg(c.ident32)), RPT)
            vtt(P, c.lt1[:, ci * 128:(ci + 1) * 128],
                c.lt1[:, ci * 128:(ci + 1) * 128], c.ps_tr32[:, pb, 0:128],
                ALU.add, (RT, RPT), (RT,))
        dma(P, "sync", c.out[tsl, :], c.lt1[:, :], (RT,),
            (hreg2("out", D, 4, j * J32, (j + 1) * J32, 0, D),))

def emit_tap_z(P, c, tap):
    ap = c.taps[tap]
    for ci in range(NCH):
        for j in range(NJ8):
            jss = slice(j * J8, (j + 1) * J8)
            dma(P, "sync", ap[ci * 128:(ci + 1) * 128, jss], c.zf[:, ci, jss],
                (rZ(c, ci, j),), ())


def emit_copy_hbm(P, c, dst_ap, src_ap, name_src, nrow, ncol, esz):
    step = max(1, (1 << 20) // (ncol * esz))
    r = 0
    while r < nrow:
        r2 = min(nrow, r + step)
        dma(P, "sync", dst_ap[r:r2, :], src_ap[r:r2, :],
            (hreg2(name_src, ncol, esz, r, r2, 0, ncol),), ())
        r = r2


PHASES_ALL = ("ln0", "l0a", "spill", "l0ffn", "l1attn", "l1ffn", "final")


def build_program(phases=PHASES_ALL, taps=(), acc_kind="Internal",
                  acc_taps=(), n_layers=4, scr_kind="Internal"):
    nc = bass.Bass("TRN2", target_bir_lowering=False, debug=False)
    c = Ctx()
    c.scratch_kind = acc_kind
    c.scr_kind = scr_kind
    build_ctx(nc, c, taps=taps)
    for tname in acc_taps:
        c.taps["acc_" + tname] = nc.dram_tensor(
            "tap_acc_" + tname, [T, D] if tname == "tm" else [D, T], F32,
            kind="ExternalOutput").ap()
    P = Prog()
    stack = contextlib.ExitStack()
    sems = {s: stack.enter_context(nc.semaphore(s)) for s in ALLSEMS}
    phase_load_consts(P, c)
    if "ln0" in phases:
        phase_load_ln0(P, c)
        if "xn" in c.taps:
            emit_tap_z(P, c, "xn")
    if "l0a" in phases:
        phase_l0_weights(P, c, 0)
        for i in range(n_layers):
            phase_l0_layer(P, c, i)
            if i + 1 < n_layers:
                phase_l0_weights(P, c, i + 1)
            tp = "z%d" % (i + 1)
            if tp in c.taps:
                emit_tap_z(P, c, tp)
    if "spill" in phases:
        phase_spill_z4(P, c)
    if "l0ffn" in phases:
        phase_l0_ffn(P, c)
    if "l1attn" in phases:
        phase_l1_attn(P, c)
    if "l1ffn" in phases:
        phase_l1_ffn(P, c)
    if "final" in phases:
        phase_final(P, c)
    if "acc_tm" in c.taps:
        emit_copy_hbm(P, c, c.taps["acc_tm"], c.acc_tm, "acc_tm", T, D, 4)
    if "acc_fm" in c.taps:
        emit_copy_hbm(P, c, c.taps["acc_fm"], c.acc_fm, "acc_fm", D, T, 4)
    P.finalize()
    P.replay(nc, sems)
    stack.close()
    return nc, c, P


def build_l0_program(n_layers=4, taps=(), scratch_kind="ExternalOutput",
                     stop_after=None, include_ffn=False):
    nc = bass.Bass("TRN2", target_bir_lowering=False, debug=False)
    c = Ctx()
    c.scratch_kind = scratch_kind
    build_ctx(nc, c, taps=taps)
    P = Prog()
    stack = contextlib.ExitStack()
    sems = {s: stack.enter_context(nc.semaphore(s)) for s in ALLSEMS}
    phase_load_consts(P, c)
    if stop_after == "consts":
        P.finalize(); P.replay(nc, sems); stack.close(); return nc, P
    phase_load_ln0(P, c, sub=(int(stop_after[3:]) if stop_after and stop_after.startswith("ln0") and len(stop_after) > 3 else 4))
    if stop_after and stop_after.startswith("ln0"):
        P.finalize(); P.replay(nc, sems); stack.close(); return nc, P
    if "xn" in c.taps:
        emit_tap_z(P, c, "xn")
    phase_l0_weights(P, c, 0)
    for i in range(n_layers):
        phase_l0_layer(P, c, i)
        if i + 1 < n_layers:
            phase_l0_weights(P, c, i + 1)
        tp = "z%d" % (i + 1)
        if tp in c.taps:
            emit_tap_z(P, c, tp)
    phase_spill_z4(P, c)
    if include_ffn:
        phase_l0_ffn(P, c)
    P.finalize()
    P.replay(nc, sems)
    stack.close()
    return nc, P


def host_cols(inp):
    colmap, colspan, ncols = build_colmap()
    out = np.zeros((128, ncols), dtype=np.float32)

    def put(name, i, vec):
        base = colmap[name] + i * colspan[name]
        v = np.ascontiguousarray(np.asarray(vec, dtype=np.float32)).reshape(-1, 128).T
        out[:, base:base + v.shape[1]] = v

    for i in range(4):
        put("l0_ln1_g", i, inp["L0_attn_ln1_g"][i])
        put("l0_ln1_b", i, inp["L0_attn_ln1_b"][i])
        put("l0_ln2_g", i, inp["L0_attn_ln2_g"][i])
        put("l0_ln2_b", i, inp["L0_attn_ln2_b"][i])
        put("l0_bqk", i, np.concatenate([np.asarray(inp["L0_attn_bq"][i]),
                                         np.asarray(inp["L0_attn_bk"][i])]))
        put("l0_bp", i, inp["L0_attn_bp"][i])
        put("l0_bf", i, inp["L0_attn_bf"][i])
        put("l0_b1", i, inp["L0_attn_b1"][i])
        put("l0_b2", i, inp["L0_attn_b2"][i])
    put("l1_bq", 0, inp["L1_attn_bq"])
    put("l1_bk", 0, inp["L1_attn_bk"])
    put("l1_sh_b1", 0, inp["L1_ffn_shared_b1"])
    put("l1_sh_b2", 0, inp["L1_ffn_shared_b2"])
    for e in range(8):
        put("l1_sp_b1", e, inp["L1_ffn_spec_b1"][e])
        put("l1_sp_b2", e, inp["L1_ffn_spec_b2"][e])
    return out


def host_inputs(inp, b):
    f32 = lambda a: np.asarray(a, np.float32)
    cols = host_cols(inp)
    wg = np.ascontiguousarray(
        f32(inp["L0_attn_w_g"]).reshape(4, 2, 128).transpose(2, 0, 1).reshape(128, 8))
    return {
        "x": np.ascontiguousarray(f32(inp["L0_attn_x_in"][b])),
        "l0_Wq": f32(inp["L0_attn_Wq"]), "l0_Wk": f32(inp["L0_attn_Wk"]),
        "l0_Wp": f32(inp["L0_attn_Wp"]), "l0_Wf": f32(inp["L0_attn_Wf"]),
        "l0_W1": f32(inp["L0_attn_W1"]), "l0_W2": f32(inp["L0_attn_W2"]),
        "f0_rwT": np.ascontiguousarray(f32(inp["L0_ffn_router_w"]).T),
        "f0_w1": f32(inp["L0_ffn_w1"]), "f0_w2": f32(inp["L0_ffn_w2"]),
        "l1_Wq": f32(inp["L1_attn_Wq"]), "l1_Wk": f32(inp["L1_attn_Wk"]),
        "f1_rwT": np.ascontiguousarray(f32(inp["L1_ffn_router_w"]).T),
        "f1_w1T": np.ascontiguousarray(np.concatenate(
            [f32(inp["L1_ffn_shared_w1"]).T[None],
             f32(inp["L1_ffn_spec_w1"]).transpose(0, 2, 1)], 0)),
        "f1_w2T": np.ascontiguousarray(np.concatenate(
            [f32(inp["L1_ffn_shared_w2"]).T[None],
             f32(inp["L1_ffn_spec_w2"]).transpose(0, 2, 1)], 0)),
        "cols": cols,
        "l0_wg_cols": wg,
        "f1_spb2": f32(inp["L1_ffn_spec_b2"]),
        "f1_rb": f32(inp["L1_ffn_router_b"]).reshape(1, 8),
    }


_CACHE = {}


def kernel(**inputs):
    if "nc" not in _CACHE:
        _CACHE["nc"], _CACHE["c"], _CACHE["P"] = build_program()
    nc = _CACHE["nc"]
    in_maps = [host_inputs(inputs, b) for b in range(8)]
    res = run_bass_kernel_spmd(nc, in_maps, core_ids=list(range(8)))
    return np.stack([np.asarray(res.results[b]["out"]) for b in range(8)])
